# version 04
# baseline (speedup 1.0000x reference)
import contextlib
import numpy as np
import ml_dtypes

import concourse.bass as bass
import concourse.mybir as mybir
from concourse.alu_op_type import AluOpType as OP
from concourse.bass_utils import run_bass_kernel_spmd

F32 = mybir.dt.float32
BF16 = mybir.dt.bfloat16
AF = mybir.ActivationFunctionType

S = 2048
E = 4096
NCHUNK = 32
H = 32
KV = 8
NQST = 4
TW = 512
EPS = 1e-5
INV_SQRT_D = float(1.0 / np.sqrt(128.0))

CW1 = float(np.float32(2.0 * np.pi))
CW2 = float(np.float32(2.0 * np.pi - np.float64(CW1)))
CW3 = float(np.float32(2.0 * np.pi - np.float64(CW1) - np.float64(CW2)))
PI_F = float(np.float32(np.pi))

KT_SCHED = {
    0: [(8 + j, j) for j in range(4)] + [(12 + j, None) for j in range(4)],
    1: [(12 + j, j) for j in range(4)],
    2: [(j, 4 + j) for j in range(4)],
    3: [(j, None) for j in range(4)] + [(4 + j, 4 + j) for j in range(4)],
}


class Plan:
    def __init__(self):
        self.ops = []
        self.count = {}

    def emit(self, eng, fn, waits=(), inc=None, inc_amt=1):
        w = tuple((s, v) for (s, v) in waits if s is not None and v)
        tok = None
        if inc is not None:
            self.count[inc] = self.count.get(inc, 0) + inc_amt
            tok = self.count[inc]
        self.ops.append((eng, w, fn, inc, inc_amt))
        return tok

    def runner(self, name, sems):
        ops = self.ops

        def run(eng):
            hw = {}
            for (engine, waits, fn, inc, inc_amt) in ops:
                if engine != name:
                    continue
                for (sname, v) in waits:
                    if hw.get(sname, 0) >= v:
                        continue
                    eng.wait_ge(sems[sname], v)
                    hw[sname] = v
                inst = fn(eng)
                if inc is not None:
                    inst.then_inc(sems[inc], inc_amt)
        return run


def build_program():
    nc = bass.Bass("TRN2", target_bir_lowering=False)
    p = Plan()

    xT = nc.dram_tensor("xT", [E, S], F32, kind="ExternalInput")
    wq = nc.dram_tensor("wq", [H, 128, E], F32, kind="ExternalInput")
    wk = nc.dram_tensor("wk", [KV, 128, E], F32, kind="ExternalInput")
    wvt = nc.dram_tensor("wvt", [NCHUNK, 128, 1024], F32, kind="ExternalInput")
    wo = nc.dram_tensor("wo", [NCHUNK, 128, E], F32, kind="ExternalInput")
    wf = nc.dram_tensor("wf", [NCHUNK, 128, E], F32, kind="ExternalInput")
    wa = nc.dram_tensor("wa", [128, E], F32, kind="ExternalInput")
    wb = nc.dram_tensor("wb", [128, E], F32, kind="ExternalInput")
    bias_t = nc.dram_tensor("bias_t", [128, NCHUNK], F32, kind="ExternalInput")
    coef = nc.dram_tensor("coef", [128, 1], F32, kind="ExternalInput")
    rt = nc.dram_tensor("rt", [128, 128], BF16, kind="ExternalInput")
    masks = nc.dram_tensor("masks", [128, 8 * TW], mybir.dt.uint8, kind="ExternalInput")
    k1 = nc.dram_tensor("k1", [128, S], F32, kind="ExternalInput")
    k2 = nc.dram_tensor("k2", [128, S], F32, kind="ExternalInput")
    outT = nc.dram_tensor("outT", [E, S], F32, kind="ExternalOutput")
    wq_bf = nc.dram_tensor("wq_bf", [H, 128, E], BF16)
    wow_bf = nc.dram_tensor("wow_bf", [NCHUNK, 128, 2 * E], BF16)
    xn_scr = nc.dram_tensor("xn_scr", [NCHUNK, 128, S], BF16)

    es = contextlib.ExitStack()
    with es:
        sbt = lambda name, shape, dt: es.enter_context(nc.sbuf_tensor(name, shape, dt))
        cos_t = sbt("cos_t", [128, S], F32)
        sin_t = sbt("sin_t", [128, S], F32)
        rt_sb = sbt("rt_sb", [128, 128], BF16)
        ones_l = sbt("ones_l", [128, 1], BF16)
        ones_b = sbt("ones_b", [1, 128], F32)
        masks_sb = sbt("masks_sb", [128, 8 * TW], mybir.dt.uint8)
        bias_sb = sbt("bias_sb", [128, NCHUNK], F32)
        coef_sb = sbt("coef_sb", [128, 1], F32)
        K_sb = sbt("K_sb", [128, KV * S], BF16)
        V_sb = sbt("V_sb", [128, 16 * 1024], BF16)
        wbig = [sbt(f"wbig{i}", [128, E], BF16) for i in range(2)]
        wkv = [sbt(f"wkv{i}", [128, E], BF16) for i in range(2)]
        x_g = [sbt(f"x_g{i}", [128, 4 * TW], BF16) for i in range(2)]
        xn_sub = sbt("xn_sub", [128, NCHUNK * TW], BF16)
        attn_t = sbt("attn_t", [128, H * TW], BF16)
        t_f32 = sbt("t_f32", [128, TW], F32)
        kc_f32 = sbt("kc_f32", [128, TW], F32)
        km_f32 = sbt("km_f32", [128, TW], F32)
        q_sb = sbt("q_sb", [128, TW], BF16)
        q_rope = sbt("q_rope", [128, TW], BF16)
        p_sb = sbt("p_sb", [128, 4 * TW], BF16)
        linv_b = sbt("linv_b", [128, TW], F32)
        x_res = sbt("x_res", [128, TW], F32)
        out_sb = sbt("out_sb", [128, TW], F32)
        eps_c = sbt("eps_c", [128, 1], F32)
        ktab_s, freq_s, red_s = t_f32, kc_f32, km_f32
        sq_s = q_sb
        k_tmp = q_rope
        linv_r = t_f32[0:1, :]
        h_sb = p_sb[:, 0:TW]
        mu_r = x_res[0:1, :]
        b_r = out_sb[0:1, :]
        c_r = linv_b[0:1, :]
        invf = linv_b[:, 0:1]

        pq = es.enter_context(nc.psum_tensor("pq", [128, 512], F32))
        prot = es.enter_context(nc.psum_tensor("prot", [128, 512], F32))
        psc0 = es.enter_context(nc.psum_tensor("psc0", [128, 1024], F32))
        psc1 = es.enter_context(nc.psum_tensor("psc1", [128, 1024], F32))
        pl = es.enter_context(nc.psum_tensor("pl", [128, 512], F32))
        pattn = es.enter_context(nc.psum_tensor("pattn", [128, 512], F32))
        psc = [psc0, psc1]

        def dma(dst, src):
            return lambda eng, d=dst, s=src: eng.dma_start(d, s)

        t_c = 0
        for dst, src in ((rt_sb[:, :], rt[:, :]), (masks_sb[:, :], masks[:, :]),
                         (bias_sb[:, :], bias_t[:, :]), (coef_sb[:, :], coef[:, :])):
            t_c = p.emit("y", dma(dst, src), inc="sd_const", inc_amt=16)

        t_ones_l = p.emit("v", lambda e: e.memset(ones_l[:, :], 1.0), inc="ve")
        t_eps = p.emit("v", lambda e: e.memset(eps_c[:, :], EPS), inc="ve")
        t_ones_b = p.emit("v", lambda e: e.memset(ones_b[:, :], 1.0), inc="ve")

        t_iota_d = p.emit("g", lambda e: e.iota(
            invf[:, :], pattern=[[0, 1]], base=0, channel_multiplier=1,
            allow_small_or_imprecise_dtypes=True), inc="po")
        c_exp = float(-np.log(10000.0) / 64.0)
        t_invf = p.emit("s", lambda e: e.activation(invf[:, :], invf[:, :], AF.Exp, scale=c_exp),
                        waits=[("po", t_iota_d)], inc="sc")
        t_invf2 = p.emit("v", lambda e: e.tensor_scalar(
            invf[64:128, :], invf[64:128, :], 10000.0, None, OP.mult),
            waits=[("sc", t_invf)], inc="ve")

        def cascade(extra_waits, tok0):
            tok = tok0
            for i, cc in enumerate((CW1, CW2, CW3)):
                use_src = (i == 0)
                def f(e, cc=cc, use_src=use_src):
                    return e.scalar_tensor_tensor(
                        red_s[:, :], ktab_s[:, :], float(-cc),
                        freq_s[:, :] if use_src else red_s[:, :], OP.mult, OP.add)
                w = [("ve", tok)] + (list(extra_waits) if i == 0 else [])
                tok = p.emit("v", f, waits=w, inc="ve")
            return p.emit("v", lambda e: e.tensor_scalar(
                red_s[:, :], red_s[:, :], PI_F, -PI_F, OP.min, OP.max),
                waits=[("ve", tok)], inc="ve")

        t_sin_done = 0
        t_cos_done = 0
        t_prev_sin_sc = 0
        t_prev_ve = t_invf2
        t_prev_k = 0
        for st in range(4):
            cols = slice(st * TW, (st + 1) * TW)
            t_i = p.emit("g", lambda e, st=st: e.iota(
                freq_s[:, :], pattern=[[1, TW]], base=st * TW, channel_multiplier=0,
                allow_small_or_imprecise_dtypes=True),
                waits=[("ve", t_prev_ve)], inc="po")
            t_fr = p.emit("v", lambda e: e.tensor_scalar(
                freq_s[:, :], freq_s[:, :], invf[:, 0:1], None, OP.mult),
                waits=[("po", t_i), ("ve", t_invf2)], inc="ve")
            t_k1 = p.emit("y", dma(ktab_s[:, :], k1[:, cols]),
                          waits=[("ve", t_prev_k)], inc="sd_ktab", inc_amt=16)
            t_r1 = cascade([("sd_ktab", t_k1), ("sc", t_prev_sin_sc)], t_fr)
            t_sin_done = p.emit("s", lambda e, cols=cols: e.activation(
                sin_t[:, cols], red_s[:, :], AF.Sin),
                waits=[("ve", t_r1), ("sc", t_prev_sin_sc)], inc="sc")
            t_f2 = p.emit("v", lambda e: e.tensor_scalar(
                freq_s[:, :], freq_s[:, :], float(np.pi / 2), None, OP.add),
                waits=[("ve", t_r1)], inc="ve")
            t_k2 = p.emit("y", dma(ktab_s[:, :], k2[:, cols]),
                          waits=[("ve", t_r1)], inc="sd_ktab", inc_amt=16)
            t_r2 = cascade([("sd_ktab", t_k2), ("sc", t_sin_done)], t_f2)
            t_cos_done = p.emit("s", lambda e, cols=cols: e.activation(
                cos_t[:, cols], red_s[:, :], AF.Sin),
                waits=[("ve", t_r2)], inc="sc")
            t_prev_sin_sc = t_cos_done
            t_prev_ve = t_r2
            t_prev_k = t_r2
        t_tables = t_cos_done

        tok_stats = {}
        tok_sq = {}
        tok_xg = {}
        t_xn_grp = {}
        tok_xnw = {}
        tok_vcons = {}
        t_vevac_all = 0
        t_vevac_all_prev = 0
        kr = {}
        kv_gen = 0
        tok_wkv = {}
        tok_wkv_used = {}
        t1_of = {}
        t2_of = {}
        last_sq_g = [0]
        last_ssq_g = [0]

        def xg_load(slot, grp, sub):
            rows = slice(grp * 512, (grp + 1) * 512)
            cols = slice(sub * TW, (sub + 1) * TW)
            src = xT[rows, cols].rearrange("(c p) s -> p c s", p=128)
            dst = x_g[slot][:, :].rearrange("p (c s) -> p c s", s=TW)
            return lambda e, d=dst, s=src: e.dma_start(d, s)

        NG = 8

        def emit_xg_load(sub, pas, grp, gates):
            slot = grp % 2
            tok_xg[(sub, pas, grp)] = p.emit(
                "g", xg_load(slot, grp, sub), waits=gates,
                inc=f"sd_xg{slot}", inc_amt=16)

        def emit_stats_grp(sub, grp):
            slot = grp % 2
            wl = [(f"sd_xg{slot}", tok_xg[(sub, "A", grp)])]
            last = 0
            for c in range(4):
                cc = grp * 4 + c
                def f_sum(e, slot=slot, c=c, cc=cc):
                    return e.matmul(pl[0:1, :], ones_l[:, :],
                                    x_g[slot][:, c * TW:(c + 1) * TW],
                                    start=(cc == 0), stop=(cc == 31))
                w = list(wl) if c == 0 else []
                if cc == 0:
                    w += [("ve", t_ones_l)]
                    if sub > 0:
                        w += [("ve", t1_of[sub - 1])]
                p.emit("t", f_sum, waits=w)
                wsq = list(wl) if c == 0 else []
                if last_ssq_g[0]:
                    wsq.append(("pe", last_ssq_g[0]))
                wsq.append(("ve", last_sq_g[0]))
                last_sq_g[0] = p.emit("v", lambda e, slot=slot, c=c: e.tensor_tensor(
                    sq_s[:, :], x_g[slot][:, c * TW:(c + 1) * TW],
                    x_g[slot][:, c * TW:(c + 1) * TW], OP.mult),
                    waits=[x for x in wsq if x[1]], inc="ve")
                def f_ssq(e, cc=cc):
                    return e.matmul(prot[0:1, :], ones_l[:, :],
                                    sq_s[:, :], start=(cc == 0), stop=(cc == 31))
                w2 = [("ve", last_sq_g[0])]
                if cc == 0 and sub > 0:
                    w2.append(("ve", t_xn_grp[(sub - 1, NG - 1)]))
                last_ssq_g[0] = p.emit("t", f_ssq, waits=w2, inc="pe")
                last = last_ssq_g[0]
            tok_stats[(sub, grp)] = last
            tok_sq[(sub, grp)] = last_sq_g[0]
            if grp + 2 < NG:
                emit_xg_load(sub, "A", grp + 2,
                             [("pe", tok_stats[(sub, grp)]),
                              ("ve", tok_sq[(sub, grp)])])

        early_wq = {}
        for h0 in (0, 1):
            early_wq[h0] = p.emit("g", dma(wbig[h0][:, :], wq[h0, :, :]),
                                  inc=f"sd_wbg{h0}", inc_amt=16)

        for grp in (0, 1):
            emit_xg_load(0, "A", grp, [])
        for grp in range(NG):
            emit_stats_grp(0, grp)

        bc_toks = {}
        xn_chunk_tok = {}

        def emit_rows(sub):
            t_stats = tok_stats[(sub, NG - 1)]
            t1 = p.emit("v", lambda e: e.tensor_scalar(
                mu_r[:, :], pl[0:1, :], float(1.0 / E), None, OP.mult),
                waits=[("pe", t_stats), ("sc", t_tables)], inc="ve")
            t1_of[sub] = t1
            t2 = p.emit("v", lambda e: e.tensor_scalar(
                b_r[:, :], prot[0:1, :], float(1.0 / E), None, OP.mult),
                waits=[("ve", t1)], inc="ve")
            t2_of[sub] = t2
            t3 = p.emit("v", lambda e: e.tensor_tensor(
                c_r[:, :], mu_r[:, :], mu_r[:, :], OP.mult),
                waits=[("ve", t2)], inc="ve")
            t4 = p.emit("v", lambda e: e.tensor_tensor(
                b_r[:, :], b_r[:, :], c_r[:, :], OP.subtract),
                waits=[("ve", t3)], inc="ve")
            t5 = p.emit("s", lambda e: e.activation(
                b_r[:, :], b_r[:, :], AF.Ln, bias=eps_c[0:1, 0:1]),
                waits=[("ve", t4), ("ve", t_eps), ("sc", t_tables)], inc="sc")
            t6 = p.emit("s", lambda e: e.activation(
                b_r[:, :], b_r[:, :], AF.Exp, scale=-0.5),
                waits=[("sc", t5)], inc="sc")
            t7 = p.emit("v", lambda e: e.tensor_tensor(
                c_r[:, :], mu_r[:, :], b_r[:, :], OP.mult),
                waits=[("sc", t6)], inc="ve")
            bc_toks[(sub, "t7")] = t7
            for grp in (0, 1):
                emit_xg_load(sub, "B", grp,
                             [("pe", tok_stats[(sub, NG - 2 + grp)]),
                              ("ve", tok_sq[(sub, NG - 2 + grp)])])

        def emit_bc(sub):
            t7 = bc_toks[(sub, "t7")]
            wbc = [("ve", t7), ("ve", t_ones_b)]
            if sub > 0:
                wbc.append(("ve", t_xn_grp[(sub - 1, NG - 1)]))
                wbc.append(("ve", kr[("m", sub - 1, KV - 1)]))
            bc_toks[(sub, "r")] = p.emit("t", lambda e: e.matmul(
                pq[:, :], ones_b[:, :], b_r[:, :], start=True, stop=True),
                waits=wbc, inc="pe")
            bc_toks[(sub, "mr")] = p.emit("t", lambda e: e.matmul(
                prot[:, :], ones_b[:, :], c_r[:, :], start=True, stop=True),
                waits=[("ve", t7), ("ve", t2_of[sub])], inc="pe")

        def emit_xn_grp(sub, grp, pe_gate):
            slot = grp % 2
            wl = [(f"sd_xg{slot}", tok_xg[(sub, "B", grp)]),
                  ("pe", bc_toks[(sub, "r")]), ("pe", bc_toks[(sub, "mr")]),
                  ("sc", t_tables)]
            if pe_gate:
                wl.append(("pe", pe_gate))
            prev_xn = xn_chunk_tok.get((sub, grp * 4 - 1), 0)
            for c in range(4):
                cc = grp * 4 + c
                w = list(wl) if c == 0 else []
                if prev_xn:
                    w.append(("ve", prev_xn))
                tt = p.emit("v", lambda e, slot=slot, c=c: e.tensor_tensor(
                    t_f32[:, :], x_g[slot][:, c * TW:(c + 1) * TW],
                    pq[:, :], OP.mult), waits=[x for x in w if x[1]], inc="ve")
                wx = [("ve", tt)]
                if cc == 0 and sub > 0:
                    wx.append(("sd_xnw", tok_xnw[sub - 1]))
                prev_xn = p.emit("v", lambda e, cc=cc: e.tensor_tensor(
                    xn_sub[:, cc * TW:(cc + 1) * TW], t_f32[:, :],
                    prot[:, :], OP.subtract), waits=wx, inc="ve")
                xn_chunk_tok[(sub, cc)] = prev_xn
            t_xn_grp[(sub, grp)] = prev_xn
            if grp + 2 < NG:
                emit_xg_load(sub, "B", grp + 2,
                             [("ve", t_xn_grp[(sub, grp)])])

        def emit_xnw(sub):
            def f_xnw(e, sub=sub):
                dst = xn_scr[:, :, sub * TW:(sub + 1) * TW].rearrange("c p s -> p c s")
                src = xn_sub[:, :].rearrange("p (c s) -> p c s", s=TW)
                return e.dma_start(dst, src)
            tok_xnw[sub] = p.emit("y", f_xnw,
                                  waits=[("ve", t_xn_grp[(sub, NG - 1)])],
                                  inc="sd_xnw", inc_amt=16)

        emit_rows(0)
        emit_bc(0)
        for grp in range(NG):
            emit_xn_grp(0, grp, 0)
        emit_xnw(0)

        for sub in range(4):
            cols = slice(sub * TW, (sub + 1) * TW)
            t_xn_done = t_xn_grp[(sub, NG - 1)]

            for oc in range(KV):
                slot = kv_gen % 2
                gates = []
                if kv_gen >= 2:
                    gates.append(("pe", tok_wkv_used[kv_gen - 2]))
                tok_wkv[kv_gen] = p.emit("g", dma(wkv[slot][:, :], wk[oc, :, :]),
                                         waits=gates, inc=f"sd_kv{slot}", inc_amt=16)
                this_gen = kv_gen
                kv_gen += 1
                kp = pattn[:, :] if oc % 2 == 0 else psc1[:, 0:512]
                last = 0
                for c in range(NCHUNK):
                    w = []
                    if oc == 0:
                        w.append(("ve", xn_chunk_tok[(sub, c)]))
                    if c == 0:
                        w += [(f"sd_kv{slot}", tok_wkv[this_gen])]
                        if oc >= 2:
                            w.append(("sc", kr[("sc", sub, oc - 2)]))
                        elif sub > 0:
                            w.append(("sc", kr[("sc", sub - 1, KV - 2 + oc)]))
                            w.append(("sc", t_vevac_all_prev))
                        if oc == 1:
                            w.append(("ve", t_xn_done))
                    last = p.emit("t", lambda e, slot=slot, c=c, kp=kp: e.matmul(
                        kp, wkv[slot][:, c * 128:(c + 1) * 128],
                        xn_sub[:, c * TW:(c + 1) * TW],
                        start=(c == 0), stop=(c == NCHUNK - 1)),
                        waits=w, inc="pe" if c == NCHUNK - 1 else None)
                tok_wkv_used[this_gen] = last
                prev_rot = kr.get(("rot", sub, oc - 1), kr.get(("rot", sub - 1, KV - 1), 0))
                prev_c1 = kr.get(("c1", sub, oc - 1), kr.get(("c1", sub - 1, KV - 1), 0))
                prev_m = kr.get(("m", sub, oc - 1), kr.get(("m", sub - 1, KV - 1), 0))
                prev_add = kr.get(("add", sub, oc - 1), kr.get(("add", sub - 1, KV - 1), 0))
                kr[("sc", sub, oc)] = p.emit(
                    "s", lambda e, kp=kp: e.activation(k_tmp[:, :], kp, AF.Copy),
                    waits=[x for x in [("pe", last), ("pe", prev_rot), ("ve", prev_c1)] if x[1]],
                    inc="sc")
                w_rot = [("sc", kr[("sc", sub, oc)]), ("sd_const", t_c), ("ve", prev_m)]
                kr[("rot", sub, oc)] = p.emit(
                    "t", lambda e: e.matmul(pq[:, :], rt_sb[:, :], k_tmp[:, :],
                                            start=True, stop=True),
                    waits=[x for x in w_rot if x[1]], inc="pe")
                kr[("c1", sub, oc)] = p.emit(
                    "v", lambda e, cols=cols: e.tensor_tensor(
                        kc_f32[:, :], k_tmp[:, :], cos_t[:, cols], OP.mult),
                    waits=[x for x in [("sc", kr[("sc", sub, oc)]), ("sc", t_tables),
                                       ("ve", prev_add)] if x[1]], inc="ve")
                kr[("m", sub, oc)] = p.emit(
                    "v", lambda e, cols=cols: e.tensor_tensor(
                        km_f32[:, :], pq[:, :], sin_t[:, cols], OP.mult),
                    waits=[("pe", kr[("rot", sub, oc)]), ("ve", kr[("c1", sub, oc)])],
                    inc="ve")
                kr[("add", sub, oc)] = p.emit(
                    "v", lambda e, oc=oc, sub=sub: e.tensor_tensor(
                        K_sb[:, oc * S + sub * TW: oc * S + (sub + 1) * TW],
                        kc_f32[:, :], km_f32[:, :], OP.add),
                    waits=[("ve", kr[("m", sub, oc)])], inc="ve")
                if sub + 1 < 4:
                    if oc == 0:
                        for grp in (0, 1):
                            emit_xg_load(sub + 1, "A", grp,
                                         [("ve", t_xn_grp[(sub, NG - 2 + grp)])])
                    if oc % 2 == 1:
                        emit_stats_grp(sub + 1, oc // 2)

            vps = [psc0[:, 0:512], psc0[:, 512:1024], psc1[:, 0:512], psc1[:, 512:1024]]
            t_vev = {}
            for half in range(2):
                wv_tok = {}
                wv_slot = {}
                wv_genid = {}

                def emit_wv_load(wg, half):
                    nonlocal kv_gen
                    slot = kv_gen % 2
                    gates = []
                    if kv_gen >= 2:
                        gates.append(("pe", tok_wkv_used[kv_gen - 2]))
                    def f_wv(e, wg=wg, half=half, slot=slot):
                        src = wvt[wg * 8:(wg + 1) * 8, :, half * 512:(half + 1) * 512]
                        dst = wkv[slot][:, :].rearrange("p (c s) -> p c s", s=512)
                        return e.dma_start(dst, src.rearrange("c p s -> p c s"))
                    wv_tok[wg] = p.emit("g", f_wv, waits=gates,
                                        inc=f"sd_kv{slot}", inc_amt=16)
                    wv_slot[wg] = slot
                    wv_genid[wg] = kv_gen
                    kv_gen += 1

                for wg in (0, 1):
                    emit_wv_load(wg, half)
                last_mm = 0
                for wg in range(4):
                    slot = wv_slot[wg]
                    for c8 in range(8):
                        c = wg * 8 + c8
                        for ts in range(4):
                            w = []
                            if c8 == 0 and ts == 0:
                                w = [(f"sd_kv{slot}", wv_tok[wg])]
                                if c == 0:
                                    if half == 0:
                                        w += [("ve", t2_of[sub]), ("sc", kr[("sc", sub, KV - 1)]),
                                              ("ve", t_xn_done)]
                                    else:
                                        w += [("sc", t_vev[(0, 3)])]
                            inc = "pe" if (c8 == 7 and ts == 3) else None
                            r = p.emit("t", lambda e, slot=slot, c8=c8, c=c, ts=ts, vp=vps[ts]: e.matmul(
                                vp, xn_sub[:, c * TW + ts * 128: c * TW + (ts + 1) * 128],
                                wkv[slot][:, c8 * 512:(c8 + 1) * 512],
                                start=(c == 0), stop=(c == NCHUNK - 1)),
                                waits=w, inc=inc)
                            if r:
                                last_mm = r
                    tok_wkv_used[wv_genid[wg]] = last_mm
                    if wg + 2 < 4:
                        emit_wv_load(wg + 2, half)
                    if sub + 1 < 4:
                        if half == 0:
                            emit_stats_grp(sub + 1, 4 + wg)
                            if wg == 3:
                                emit_rows(sub + 1)
                        else:
                            if wg == 0:
                                emit_bc(sub + 1)
                            emit_xn_grp(sub + 1, 2 * wg, last_mm)
                            emit_xn_grp(sub + 1, 2 * wg + 1, 0)
                for ts in range(4):
                    kt_glob = sub * 4 + ts
                    t_vev[(half, ts)] = p.emit(
                        "s", lambda e, vp=vps[ts], half=half, kt_glob=kt_glob: e.activation(
                            V_sb[:, kt_glob * 1024 + half * 512: kt_glob * 1024 + (half + 1) * 512],
                            vp, AF.Copy),
                        waits=[("pe", last_mm)], inc="sc")
            t_vevac_all = t_vev[(1, 3)]
            t_vevac_all_prev = t_vevac_all
            tok_vcons[sub] = last_mm
            if sub + 1 < 4:
                emit_xnw(sub + 1)

        w_lora_gate = [("pe", tok_wkv_used[kv_gen - 1]), ("pe", tok_wkv_used[kv_gen - 2])]
        p.emit("g", dma(wkv[0][:, :], wa[:, :]), waits=w_lora_gate,
               inc="sd_lora", inc_amt=16)
        t_lora = p.emit("g", dma(wkv[1][:, :], wb[:, :]), inc="sd_lora", inc_amt=16)
        wa_s, wb_s = wkv[0], wkv[1]

        gh_q = {}
        tok_qevac = {}
        tok_rot = {}
        tok_c1 = {}
        tok_m = {}
        tok_add = {}
        tok_scores = {}
        tok_pv = {}
        tok_l = {}
        tok_recip = {}
        tok_bc = {}
        tok_bcevac = {}
        tok_attnmul = {}
        pair_exp = {}
        tok_out = {}
        tok_stt = {}
        tok_store = {}
        tok_xres = {}
        tok_wbig = {}
        wbk_q = {}
        wbk_ow = {}
        tok_xn_load = {}
        state = {"pair": 0}
        prev_use_slot = {0: 0, 1: 0}
        prev_wbk_slot = {0: 0, 1: 0}
        prev_out_last = 0
        t_hs = 0
        t_hs_prev = 0

        def wq_load(qst, h):
            slot = h % 2
            if qst == 0 and h in (0, 1):
                tok_wbig[(qst, "q", h)] = (f"sd_wbg{slot}", early_wq[h])
                return
            gates = []
            if prev_use_slot[slot]:
                gates.append(("pe", prev_use_slot[slot]))
            if prev_wbk_slot[slot]:
                gates.append((f"sd_wbk{slot}", prev_wbk_slot[slot]))
            if qst == 0:
                sem = f"sd_wbg{slot}"
                tok = p.emit("g", dma(wbig[slot][:, :], wq[h, :, :]),
                             waits=gates, inc=sem, inc_amt=16)
            else:
                sem = f"sd_wb{slot}"
                gates.append((f"sd_wbk{h % 2}", wbk_q[h]))
                tok = p.emit("y", dma(wbig[slot][:, :], wq_bf[h, :, :]),
                             waits=gates, inc=sem, inc_amt=16)
            tok_wbig[(qst, "q", h)] = (sem, tok)

        def wow_load(qst, oc, half):
            u = 2 * oc + half
            slot = u % 2
            gates = []
            if prev_use_slot[slot]:
                gates.append(("pe", prev_use_slot[slot]))
            if prev_wbk_slot[slot]:
                gates.append((f"sd_wbk{slot}", prev_wbk_slot[slot]))
            if qst == 0:
                sem = f"sd_wbg{slot}"
                src = wo[oc, :, :] if half == 0 else wf[oc, :, :]
                tok = p.emit("g", dma(wbig[slot][:, :], src),
                             waits=gates, inc=sem, inc_amt=16)
            else:
                sem = f"sd_wb{slot}"
                gates.append((f"sd_wbk{slot}", wbk_ow[u]))
                tok = p.emit("y", dma(wbig[slot][:, :],
                                      wow_bf[oc, :, half * E:(half + 1) * E]),
                             waits=gates, inc=sem, inc_amt=16)
            tok_wbig[(qst, "ow", u)] = (sem, tok)

        def xres_load(qst, oc):
            gates = []
            if (qst, oc - 1) in tok_stt:
                gates.append(("ve", tok_stt[(qst, oc - 1)]))
            elif qst > 0:
                gates.append(("ve", tok_stt[(qst - 1, NCHUNK - 1)]))
            tok_xres[(qst, oc)] = p.emit(
                "y", lambda e, qst=qst, oc=oc: e.dma_start(
                    x_res[:, :],
                    xT[oc * 128:(oc + 1) * 128, qst * TW:(qst + 1) * TW]),
                waits=gates, inc="sd_xres", inc_amt=16)

        def head_tail(qst, h, fill=None):
            gh = qst * H + h
            if fill is None:
                fill = lambda n: None
            kts = KT_SCHED[qst]
            nk = len(kts)
            npair = nk // 2
            pb = state["pair"]
            kv = h // 4
            t_exp = {}
            t_mask = {}
            pv_done = {}

            def emit_lpv(j):
                slot = (pb + j) % 2
                last_l = last_pv = 0
                for half in range(2):
                    pos = 2 * j + half
                    kt = kts[pos][0]
                    sl = slice(slot * 1024 + half * 512, slot * 1024 + (half + 1) * 512)
                    w = []
                    if half == 0:
                        w = [("sc", t_exp[j]), ("ve", t_ones_l)]
                        if t_mask.get(j):
                            w.append(("ve", t_mask[j]))
                        if j == 0:
                            if gh > 0:
                                w.append(("ve", tok_bcevac[gh - 1]))
                            else:
                                w.append(("ve", kr[("add", 3, KV - 1)]))
                            if h == 0 and qst > 0:
                                w.append(("ve", t_hs_prev))
                    last_l = p.emit("t", lambda e, sl=sl, pos=pos: e.matmul(
                        pl[0:1, :], ones_l[:, :], p_sb[:, sl],
                        start=(pos == 0), stop=(pos == nk - 1)),
                        waits=w, inc="pe" if pos == nk - 1 else None)
                    w2 = []
                    if half == 0 and j == 0:
                        w2 = [("sc", t_exp[j])]
                        if t_mask.get(j):
                            w2.append(("ve", t_mask[j]))
                        if gh > 0:
                            w2.append(("ve", tok_attnmul[gh - 1]))
                        else:
                            w2.append(("sc", kr[("sc", 3, KV - 1)]))
                    last_pv = p.emit("t", lambda e, sl=sl, pos=pos, kt=kt, kv=kv: e.matmul(
                        pattn[:, :], V_sb[:, kt * 1024 + kv * 128: kt * 1024 + (kv + 1) * 128],
                        p_sb[:, sl], start=(pos == 0), stop=(pos == nk - 1)),
                        waits=w2, inc="pe" if half == 1 else None)
                pv_done[j] = last_pv
                return last_l, last_pv

            last_l = last_pv = 0
            for j in range(npair):
                fill(4)
                pidx = pb + j
                par = pidx % 2
                slot = pidx % 2
                t_mm = 0
                for half in range(2):
                    kt = kts[2 * j + half][0]
                    w = []
                    if half == 0:
                        w = [("ve", tok_add[gh])]
                        if pidx >= 2:
                            w.append(("sc", pair_exp[pidx - 2]))
                        else:
                            w.append(("sc", t_vevac_all))
                    t_mm = p.emit("t", lambda e, par=par, half=half, kt=kt, kv=kv: e.matmul(
                        psc[par][:, half * 512:(half + 1) * 512],
                        K_sb[:, kv * S + kt * 128: kv * S + (kt + 1) * 128],
                        q_rope[:, :], start=True, stop=True),
                        waits=w, inc="pe" if half == 1 else None)
                tok_scores[gh] = t_mm
                w_e = [("pe", t_mm)]
                if j >= 2:
                    w_e.append(("pe", pv_done[j - 2]))
                elif gh > 0:
                    w_e.append(("pe", tok_pv[gh - 1]))
                t_exp[j] = p.emit("s", lambda e, par=par, slot=slot: e.activation(
                    p_sb[:, slot * 1024:(slot + 1) * 1024], psc[par][:, :], AF.Exp),
                    waits=w_e, inc="sc")
                pair_exp[pidx] = t_exp[j]
                tm = 0
                for half in range(2):
                    pos = 2 * j + half
                    mi = kts[pos][1]
                    if mi is None:
                        continue
                    sl = slice(slot * 1024 + half * 512, slot * 1024 + (half + 1) * 512)
                    w = [("sc", t_exp[j]), ("sd_const", t_c)]
                    if tm:
                        w.append(("ve", tm))
                    tm = p.emit("v", lambda e, sl=sl, mi=mi: e.tensor_tensor(
                        p_sb[:, sl], p_sb[:, sl],
                        masks_sb[:, mi * 512:(mi + 1) * 512], OP.mult),
                        waits=w, inc="ve")
                t_mask[j] = tm
                if j >= 1:
                    fill(4)
                    last_l, last_pv = emit_lpv(j - 1)
            fill(4)
            last_l, last_pv = emit_lpv(npair - 1)
            state["pair"] = pb + npair
            tok_l[gh] = last_l
            tok_pv[gh] = last_pv
            fill(8)
            tok_recip[gh] = p.emit("v", lambda e: e.reciprocal(linv_r[:, :], pl[0:1, :]),
                                   waits=[("pe", last_l)], inc="ve")
            tok_bc[gh] = p.emit("t", lambda e: e.matmul(
                pl[:, :], ones_b[:, :], linv_r[:, :], start=True, stop=True),
                waits=[("ve", tok_recip[gh]), ("ve", t_ones_b)], inc="pe")
            tok_bcevac[gh] = p.emit("v", lambda e: e.tensor_copy(linv_b[:, :], pl[:, :]),
                                    waits=[("pe", tok_bc[gh])], inc="ve")
            tok_attnmul[gh] = p.emit("v", lambda e, h=h: e.tensor_tensor(
                attn_t[:, h * TW:(h + 1) * TW], pattn[:, :], linv_b[:, :], OP.mult),
                waits=[("pe", last_pv), ("ve", tok_bcevac[gh])], inc="ve")

        for qst in range(NQST):
            cols = slice(qst * TW, (qst + 1) * TW)
            w_xn = [("sd_xnw", tok_xnw[3])]
            if qst == 0:
                w_xn.append(("pe", tok_vcons[3]))
            else:
                w_xn.append(("pe", prev_out_last))
            tok_xn_half = p.emit(
                "y", lambda e, qst=qst: e.dma_start(
                    xn_sub[:, 0:16 * TW].rearrange("p (c s) -> p c s", s=TW),
                    xn_scr[0:16, :, qst * TW:(qst + 1) * TW].rearrange("c p s -> p c s")),
                waits=w_xn, inc="sd_xnr", inc_amt=16)
            tok_xn_load[qst] = p.emit(
                "y", lambda e, qst=qst: e.dma_start(
                    xn_sub[:, 16 * TW:].rearrange("p (c s) -> p c s", s=TW),
                    xn_scr[16:32, :, qst * TW:(qst + 1) * TW].rearrange("c p s -> p c s")),
                inc="sd_xnr2", inc_amt=16)

            for h in (0, 1):
                wq_load(qst, h)

            for h in range(H):
                gh = qst * H + h
                slot = h % 2
                qstate = {"c": 0, "last": 0}

                def fill_q(n, h=h, gh=gh, slot=slot, qstate=qstate, qst=qst,
                           txh=tok_xn_half):
                    while n > 0 and qstate["c"] < NCHUNK:
                        c = qstate["c"]
                        w = []
                        if c == 0:
                            w = [tok_wbig[(qst, "q", h)], ("sd_xnr", txh)]
                            if gh > 0:
                                w.append(("sc", tok_qevac[gh - 1]))
                            if h == 0:
                                if qst == 0:
                                    w.append(("ve", kr[("m", 3, KV - 1)]))
                                else:
                                    w.append(("ve", tok_stt[(qst - 1, NCHUNK - 1)]))
                        elif c == 16:
                            w = [("sd_xnr2", tok_xn_load[qst])]
                        qstate["last"] = p.emit("t", lambda e, slot=slot, c=c: e.matmul(
                            pq[:, :], wbig[slot][:, c * 128:(c + 1) * 128],
                            xn_sub[:, c * TW:(c + 1) * TW],
                            start=(c == 0), stop=(c == NCHUNK - 1)),
                            waits=w, inc="pe" if c == NCHUNK - 1 else None)
                        qstate["c"] += 1
                        n -= 1

                if h > 0:
                    head_tail(qst, h - 1, fill_q)
                fill_q(NCHUNK)
                gh_q[gh] = qstate["last"]
                prev_use_slot[slot] = qstate["last"]

                if qst == 0:
                    wbk_q[h] = p.emit(
                        "y", dma(wq_bf[h, :, :], wbig[slot][:, :]),
                        waits=[tok_wbig[(qst, "q", h)]],
                        inc=f"sd_wbk{slot}", inc_amt=16)
                    prev_wbk_slot[slot] = wbk_q[h]
                if h + 2 < H:
                    wq_load(qst, h + 2)

                w_ev = [("pe", gh_q[gh])]
                if gh > 0:
                    w_ev += [("pe", tok_rot[gh - 1]), ("ve", tok_c1[gh - 1])]
                tok_qevac[gh] = p.emit("s", lambda e: e.activation(
                    q_sb[:, :], pq[:, :], AF.Copy, scale=INV_SQRT_D),
                    waits=w_ev, inc="sc")

                w_rot = [("sc", tok_qevac[gh])]
                if gh > 0:
                    w_rot.append(("ve", tok_m[gh - 1]))
                else:
                    w_rot.append(("ve", kr[("m", 3, KV - 1)]))
                if h == 0 and qst > 0:
                    w_rot.append(("ve", tok_stt[(qst - 1, NCHUNK - 1)]))
                tok_rot[gh] = p.emit("t", lambda e: e.matmul(
                    prot[:, :], rt_sb[:, :], q_sb[:, :], start=True, stop=True),
                    waits=w_rot, inc="pe")
                w_c1 = [("sc", tok_qevac[gh])]
                if gh > 0:
                    w_c1.append(("ve", tok_add[gh - 1]))
                else:
                    w_c1.append(("ve", kr[("add", 3, KV - 1)]))
                tok_c1[gh] = p.emit("v", lambda e, cols=cols: e.tensor_tensor(
                    kc_f32[:, :], q_sb[:, :], cos_t[:, cols], OP.mult),
                    waits=w_c1, inc="ve")
                tok_m[gh] = p.emit("v", lambda e, cols=cols: e.tensor_tensor(
                    km_f32[:, :], prot[:, :], sin_t[:, cols], OP.mult),
                    waits=[("pe", tok_rot[gh]), ("ve", tok_c1[gh])], inc="ve")
                w_add = [("ve", tok_m[gh])]
                if gh > 0:
                    w_add.append(("pe", tok_scores[gh - 1]))
                tok_add[gh] = p.emit("v", lambda e: e.tensor_tensor(
                    q_rope[:, :], kc_f32[:, :], km_f32[:, :], OP.add),
                    waits=w_add, inc="ve")

            head_tail(qst, H - 1)
            gh_last = qst * H + H - 1

            w_h = [("ve", tok_bcevac[gh_last]), ("sd_lora", t_lora),
                   ("sd_xnr", tok_xn_half), ("sd_xnr2", tok_xn_load[qst])]
            last = 0
            for c in range(NCHUNK):
                last = p.emit("t", lambda e, c=c: e.matmul(
                    pl[:, :], wa_s[:, c * 128:(c + 1) * 128],
                    xn_sub[:, c * TW:(c + 1) * TW],
                    start=(c == 0), stop=(c == NCHUNK - 1)),
                    waits=w_h if c == 0 else [], inc="pe" if c == NCHUNK - 1 else None)
            w_hs = [("pe", last), ("sd_const", t_c)]
            if qst > 0:
                w_hs.append(("pe", tok_out[(qst - 1, NCHUNK - 1)]))
            t_hs = p.emit("v", lambda e: e.tensor_scalar(
                h_sb[:, :], pl[:, :], coef_sb[:, 0:1], 2.0, OP.mult, OP.mult),
                waits=w_hs, inc="ve")
            t_hs_prev = t_hs

            for u in (0, 1):
                wow_load(qst, u // 2, u % 2)
            xres_load(qst, 0)

            for oc in range(NCHUNK):
                pout = pq if oc % 2 == 0 else prot
                u0, u1 = 2 * oc, 2 * oc + 1
                s0, s1 = u0 % 2, u1 % 2
                w0 = [tok_wbig[(qst, "ow", u0)],
                      ("ve", tok_attnmul[gh_last]), ("ve", t_hs)]
                if (qst, oc - 2) in tok_stt:
                    w0.append(("ve", tok_stt[(qst, oc - 2)]))
                else:
                    w0.append(("sc", tok_qevac[gh_last]))
                    w0.append(("ve", tok_m[gh_last]))
                last = 0
                for h in range(H):
                    last = p.emit("t", lambda e, s0=s0, h=h, pout=pout: e.matmul(
                        pout[:, :], wbig[s0][:, h * 128:(h + 1) * 128],
                        attn_t[:, h * TW:(h + 1) * TW],
                        start=(h == 0), stop=False),
                        waits=w0 if h == 0 else [], inc="pe" if h == H - 1 else None)
                prev_use_slot[s0] = last
                if qst == 0:
                    wbk_ow[u0] = p.emit(
                        "y", dma(wow_bf[oc, :, 0:E], wbig[s0][:, :]),
                        waits=[tok_wbig[(qst, "ow", u0)]],
                        inc=f"sd_wbk{s0}", inc_amt=16)
                    prev_wbk_slot[s0] = wbk_ow[u0]
                if u0 + 2 < 2 * NCHUNK:
                    wow_load(qst, (u0 + 2) // 2, (u0 + 2) % 2)

                w1 = [tok_wbig[(qst, "ow", u1)]]
                lastw = 0
                for c in range(NCHUNK):
                    lastw = p.emit("t", lambda e, s1=s1, c=c, pout=pout: e.matmul(
                        pout[:, :], wbig[s1][:, c * 128:(c + 1) * 128],
                        xn_sub[:, c * TW:(c + 1) * TW],
                        start=False, stop=False),
                        waits=w1 if c == 0 else [], inc="pe" if c == NCHUNK - 1 else None)
                prev_use_slot[s1] = lastw
                if qst == 0:
                    wbk_ow[u1] = p.emit(
                        "y", dma(wow_bf[oc, :, E:2 * E], wbig[s1][:, :]),
                        waits=[tok_wbig[(qst, "ow", u1)]],
                        inc=f"sd_wbk{s1}", inc_amt=16)
                    prev_wbk_slot[s1] = wbk_ow[u1]
                if u1 + 2 < 2 * NCHUNK:
                    wow_load(qst, (u1 + 2) // 2, (u1 + 2) % 2)

                tok_out[(qst, oc)] = p.emit("t", lambda e, oc=oc, pout=pout: e.matmul(
                    pout[:, :], wb_s[:, oc * 128:(oc + 1) * 128],
                    h_sb[:, :], start=False, stop=True),
                    waits=[("sd_lora", t_lora), ("ve", t_hs)], inc="pe")

                w_stt = [("pe", tok_out[(qst, oc)]),
                         ("sd_xres", tok_xres[(qst, oc)]),
                         ("sd_const", t_c)]
                if (qst, oc - 1) in tok_store:
                    w_stt.append(("sd_out", tok_store[(qst, oc - 1)]))
                elif qst > 0:
                    w_stt.append(("sd_out", tok_store[(qst - 1, NCHUNK - 1)]))
                tok_stt[(qst, oc)] = p.emit(
                    "v", lambda e, oc=oc, pout=pout: e.scalar_tensor_tensor(
                        out_sb[:, :], pout[:, :], bias_sb[:, oc:oc + 1],
                        x_res[:, :], OP.add, OP.add),
                    waits=w_stt, inc="ve")
                tok_store[(qst, oc)] = p.emit(
                    "y", lambda e, oc=oc, qst=qst: e.dma_start(
                        outT[oc * 128:(oc + 1) * 128, qst * TW:(qst + 1) * TW],
                        out_sb[:, :]),
                    waits=[("ve", tok_stt[(qst, oc)])], inc="sd_out", inc_amt=16)
                if oc + 1 < NCHUNK:
                    xres_load(qst, oc + 1)
            prev_out_last = tok_out[(qst, NCHUNK - 1)]

        p.emit("y", lambda e: e.nop(),
               waits=[("sd_out", tok_store[(NQST - 1, NCHUNK - 1)])])

        sem_stack = contextlib.ExitStack()
        with sem_stack:
            sems = {}
            for name in list(p.count):
                sems[name] = sem_stack.enter_context(nc.semaphore(name))
            with nc.Block() as block:
                block.tensor(p.runner("t", sems))
                block.vector(p.runner("v", sems))
                block.scalar(p.runner("s", sems))
                block.gpsimd(p.runner("g", sems))
                block.sync(p.runner("y", sems))
    return nc



def _prep_inputs(L0_attn_hidden_states, L0_attn_Wq, L0_attn_Wk, L0_attn_Wv,
                 L0_attn_Wo, L0_ffn_lora_A, L0_ffn_lora_B, L0_ffn_W,
                 L0_ffn_bias, L0_ffn_lora_weights):
    f32 = np.float32
    x = np.asarray(L0_attn_hidden_states)
    Wq = np.asarray(L0_attn_Wq); Wk = np.asarray(L0_attn_Wk)
    Wv = np.asarray(L0_attn_Wv); Wo = np.asarray(L0_attn_Wo)
    A = np.asarray(L0_ffn_lora_A); B = np.asarray(L0_ffn_lora_B)
    W = np.asarray(L0_ffn_W); bias = np.asarray(L0_ffn_bias)
    lw = np.asarray(L0_ffn_lora_weights)

    com = {}
    com["wq"] = np.ascontiguousarray(Wq.reshape(H, 128, NCHUNK, 128).transpose(0, 3, 2, 1)).reshape(H, 128, E)
    com["wk"] = np.ascontiguousarray(Wk.reshape(KV, 128, NCHUNK, 128).transpose(0, 3, 2, 1)).reshape(KV, 128, E)
    com["wvt"] = np.ascontiguousarray(Wv.T).reshape(NCHUNK, 128, 1024)
    com["wo"] = np.ascontiguousarray(Wo.reshape(NCHUNK, 128, H, 128).transpose(0, 3, 2, 1)).reshape(NCHUNK, 128, E)
    com["wf"] = np.ascontiguousarray(W.reshape(NCHUNK, 128, NCHUNK, 128).transpose(0, 3, 2, 1)).reshape(NCHUNK, 128, E)
    A_all = A.reshape(128, E)
    com["wa"] = np.ascontiguousarray(
        A_all.T.reshape(NCHUNK, 128, 128).transpose(1, 0, 2)).reshape(128, E)
    com["wb"] = np.ascontiguousarray(B.transpose(0, 2, 1).reshape(128, E))
    com["bias_t"] = np.ascontiguousarray(bias.reshape(NCHUNK, 128).T)
    com["coef"] = np.ascontiguousarray(np.repeat(lw, 16).reshape(128, 1))

    R = np.zeros((128, 128), dtype=f32)
    for i in range(64):
        R[i, i + 64] = -1.0
        R[i + 64, i] = 1.0
    com["rt"] = np.ascontiguousarray(R.T.astype(ml_dtypes.bfloat16))

    kl = np.arange(128)[:, None]
    ql = np.arange(TW)[None, :]
    m = np.zeros((8, 128, TW), dtype=f32)
    for j in range(4):
        m[j] = (ql <= kl + j * 128)
        m[4 + j] = (ql >= kl + j * 128)
    com["masks"] = np.ascontiguousarray(
        m.transpose(1, 0, 2).reshape(128, 8 * TW).astype(np.uint8))

    d = np.arange(128, dtype=np.float64) % 64
    invfreq = 10000.0 ** (-d / 64.0)
    t = np.arange(S, dtype=np.float64)
    freqs = invfreq[:, None] * t[None, :]
    com["k1"] = np.ascontiguousarray(np.round(freqs / (2 * np.pi)).astype(f32))
    com["k2"] = np.ascontiguousarray(np.round((freqs + np.pi / 2) / (2 * np.pi)).astype(f32))

    per_core = [np.ascontiguousarray(x[b].T) for b in range(x.shape[0])]
    return com, per_core


_NC_CACHE = {}


def _get_nc():
    if "nc" not in _NC_CACHE:
        _NC_CACHE["nc"] = build_program()
    return _NC_CACHE["nc"]


def kernel(**inputs):
    com, per_core = _prep_inputs(**inputs)
    nc = _get_nc()
    in_maps = []
    for b in range(len(per_core)):
        m = dict(com)
        m["xT"] = per_core[b]
        in_maps.append(m)
    res = run_bass_kernel_spmd(nc, in_maps, core_ids=list(range(len(per_core))))
    outs = [np.ascontiguousarray(np.asarray(r["outT"]).T.astype(np.float32))
            for r in res.results]
    return np.stack(outs, axis=0)


# version 06
# speedup vs baseline: 1.0337x; 1.0337x over previous
import contextlib
import numpy as np
import ml_dtypes

import concourse.bass as bass
import concourse.mybir as mybir
from concourse.alu_op_type import AluOpType as OP
from concourse.bass_utils import run_bass_kernel_spmd

F32 = mybir.dt.float32
BF16 = mybir.dt.bfloat16
AF = mybir.ActivationFunctionType

S = 2048
E = 4096
NCHUNK = 32
H = 32
KV = 8
NQST = 4
TW = 512
EPS = 1e-5
INV_SQRT_D = float(1.0 / np.sqrt(128.0))

CW1 = float(np.float32(2.0 * np.pi))
CW2 = float(np.float32(2.0 * np.pi - np.float64(CW1)))
CW3 = float(np.float32(2.0 * np.pi - np.float64(CW1) - np.float64(CW2)))
PI_F = float(np.float32(np.pi))

KT_SCHED = {
    0: [(8 + j, j) for j in range(4)] + [(12 + j, None) for j in range(4)],
    1: [(12 + j, j) for j in range(4)],
    2: [(j, 4 + j) for j in range(4)],
    3: [(j, None) for j in range(4)] + [(4 + j, 4 + j) for j in range(4)],
}


class Plan:
    def __init__(self):
        self.ops = []
        self.count = {}

    def emit(self, eng, fn, waits=(), inc=None, inc_amt=1):
        w = tuple((s, v) for (s, v) in waits if s is not None and v)
        tok = None
        if inc is not None:
            self.count[inc] = self.count.get(inc, 0) + inc_amt
            tok = self.count[inc]
        self.ops.append((eng, w, fn, inc, inc_amt))
        return tok

    def runner(self, name, sems):
        ops = self.ops

        def run(eng):
            hw = {}
            for (engine, waits, fn, inc, inc_amt) in ops:
                if engine != name:
                    continue
                for (sname, v) in waits:
                    if hw.get(sname, 0) >= v:
                        continue
                    eng.wait_ge(sems[sname], v)
                    hw[sname] = v
                inst = fn(eng)
                if inc is not None:
                    inst.then_inc(sems[inc], inc_amt)
        return run


def build_program():
    nc = bass.Bass("TRN2", target_bir_lowering=False)
    p = Plan()

    xT = nc.dram_tensor("xT", [E, S], F32, kind="ExternalInput")
    wq = nc.dram_tensor("wq", [H, 128, E], F32, kind="ExternalInput")
    wk = nc.dram_tensor("wk", [KV, 128, E], F32, kind="ExternalInput")
    wvt = nc.dram_tensor("wvt", [NCHUNK, 128, 1024], F32, kind="ExternalInput")
    wo = nc.dram_tensor("wo", [NCHUNK, 128, E], F32, kind="ExternalInput")
    wf = nc.dram_tensor("wf", [NCHUNK, 128, E], F32, kind="ExternalInput")
    wa = nc.dram_tensor("wa", [128, E], F32, kind="ExternalInput")
    wb = nc.dram_tensor("wb", [128, E], F32, kind="ExternalInput")
    bias_t = nc.dram_tensor("bias_t", [128, NCHUNK], F32, kind="ExternalInput")
    coef = nc.dram_tensor("coef", [128, 1], F32, kind="ExternalInput")
    rt = nc.dram_tensor("rt", [128, 128], BF16, kind="ExternalInput")
    masks = nc.dram_tensor("masks", [128, 8 * TW], mybir.dt.uint8, kind="ExternalInput")
    k1 = nc.dram_tensor("k1", [128, S], F32, kind="ExternalInput")
    k2 = nc.dram_tensor("k2", [128, S], F32, kind="ExternalInput")
    outT = nc.dram_tensor("outT", [E, S], F32, kind="ExternalOutput")
    wq_bf = nc.dram_tensor("wq_bf", [H, 128, E], BF16)
    wk_bf = nc.dram_tensor("wk_bf", [KV, 128, E], BF16)
    wv_bf = nc.dram_tensor("wv_bf", [NCHUNK, 128, 1024], BF16)
    wow_bf = nc.dram_tensor("wow_bf", [NCHUNK, 128, 2 * E], BF16)
    xn_scr = nc.dram_tensor("xn_scr", [NCHUNK, 128, S], BF16)

    es = contextlib.ExitStack()
    with es:
        sbt = lambda name, shape, dt: es.enter_context(nc.sbuf_tensor(name, shape, dt))
        cos_t = sbt("cos_t", [128, S], F32)
        sin_t = sbt("sin_t", [128, S], F32)
        rt_sb = sbt("rt_sb", [128, 128], BF16)
        ones_l = sbt("ones_l", [128, 1], BF16)
        ones_b = sbt("ones_b", [1, 128], F32)
        masks_sb = sbt("masks_sb", [128, 8 * TW], mybir.dt.uint8)
        bias_sb = sbt("bias_sb", [128, NCHUNK], F32)
        coef_sb = sbt("coef_sb", [128, 1], F32)
        K_sb = sbt("K_sb", [128, KV * S], BF16)
        V_sb = sbt("V_sb", [128, 16 * 1024], BF16)
        wbig = [sbt(f"wbig{i}", [128, E], BF16) for i in range(2)]
        wkv = [sbt(f"wkv{i}", [128, E], BF16) for i in range(2)]
        x_g = [sbt(f"x_g{i}", [128, 4 * TW], BF16) for i in range(2)]
        xn_sub = sbt("xn_sub", [128, NCHUNK * TW], BF16)
        attn_t = sbt("attn_t", [128, H * TW], BF16)
        t_f32 = sbt("t_f32", [128, TW], F32)
        kc_f32 = sbt("kc_f32", [128, TW], F32)
        km_f32 = sbt("km_f32", [128, TW], F32)
        q_sb = sbt("q_sb", [128, TW], BF16)
        q_rope = sbt("q_rope", [128, TW], BF16)
        p_sb = sbt("p_sb", [128, 4 * TW], BF16)
        linv_b = sbt("linv_b", [128, TW], F32)
        x_res = sbt("x_res", [128, TW], F32)
        out_sb = sbt("out_sb", [128, TW], F32)
        eps_c = sbt("eps_c", [128, 1], F32)
        ktab_s, freq_s, red_s = t_f32, kc_f32, km_f32
        sq_s = q_sb
        k_tmp = q_rope
        linv_r = t_f32[0:1, :]
        h_sb = p_sb[:, 0:TW]
        mu_r = x_res[0:1, :]
        b_r = out_sb[0:1, :]
        c_r = linv_b[0:1, :]
        invf = linv_b[:, 0:1]

        pq = es.enter_context(nc.psum_tensor("pq", [128, 512], F32))
        prot = es.enter_context(nc.psum_tensor("prot", [128, 512], F32))
        psc0 = es.enter_context(nc.psum_tensor("psc0", [128, 1024], F32))
        psc1 = es.enter_context(nc.psum_tensor("psc1", [128, 1024], F32))
        pl = es.enter_context(nc.psum_tensor("pl", [128, 512], F32))
        pattn = es.enter_context(nc.psum_tensor("pattn", [128, 512], F32))
        psc = [psc0, psc1]

        def dma(dst, src):
            return lambda eng, d=dst, s=src: eng.dma_start(d, s)

        t_c = 0
        for dst, src in ((rt_sb[:, :], rt[:, :]), (masks_sb[:, :], masks[:, :]),
                         (bias_sb[:, :], bias_t[:, :]), (coef_sb[:, :], coef[:, :])):
            t_c = p.emit("y", dma(dst, src), inc="sd_const", inc_amt=16)

        t_ones_l = p.emit("v", lambda e: e.memset(ones_l[:, :], 1.0), inc="ve")
        t_eps = p.emit("v", lambda e: e.memset(eps_c[:, :], EPS), inc="ve")
        t_ones_b = p.emit("v", lambda e: e.memset(ones_b[:, :], 1.0), inc="ve")

        t_iota_d = p.emit("g", lambda e: e.iota(
            invf[:, :], pattern=[[0, 1]], base=0, channel_multiplier=1,
            allow_small_or_imprecise_dtypes=True), inc="po")
        c_exp = float(-np.log(10000.0) / 64.0)
        t_invf = p.emit("s", lambda e: e.activation(invf[:, :], invf[:, :], AF.Exp, scale=c_exp),
                        waits=[("po", t_iota_d)], inc="sc")
        t_invf2 = p.emit("v", lambda e: e.tensor_scalar(
            invf[64:128, :], invf[64:128, :], 10000.0, None, OP.mult),
            waits=[("sc", t_invf)], inc="ve")

        def cascade(extra_waits, tok0):
            tok = tok0
            for i, cc in enumerate((CW1, CW2, CW3)):
                use_src = (i == 0)
                def f(e, cc=cc, use_src=use_src):
                    return e.scalar_tensor_tensor(
                        red_s[:, :], ktab_s[:, :], float(-cc),
                        freq_s[:, :] if use_src else red_s[:, :], OP.mult, OP.add)
                w = [("ve", tok)] + (list(extra_waits) if i == 0 else [])
                tok = p.emit("v", f, waits=w, inc="ve")
            return p.emit("v", lambda e: e.tensor_scalar(
                red_s[:, :], red_s[:, :], PI_F, -PI_F, OP.min, OP.max),
                waits=[("ve", tok)], inc="ve")

        t_sin_done = 0
        t_cos_done = 0
        t_prev_sin_sc = 0
        t_prev_ve = t_invf2
        t_prev_k = 0
        for st in range(4):
            cols = slice(st * TW, (st + 1) * TW)
            t_i = p.emit("g", lambda e, st=st: e.iota(
                freq_s[:, :], pattern=[[1, TW]], base=st * TW, channel_multiplier=0,
                allow_small_or_imprecise_dtypes=True),
                waits=[("ve", t_prev_ve)], inc="po")
            t_fr = p.emit("v", lambda e: e.tensor_scalar(
                freq_s[:, :], freq_s[:, :], invf[:, 0:1], None, OP.mult),
                waits=[("po", t_i), ("ve", t_invf2)], inc="ve")
            t_k1 = p.emit("y", dma(ktab_s[:, :], k1[:, cols]),
                          waits=[("ve", t_prev_k)], inc="sd_ktab", inc_amt=16)
            t_r1 = cascade([("sd_ktab", t_k1), ("sc", t_prev_sin_sc)], t_fr)
            t_sin_done = p.emit("s", lambda e, cols=cols: e.activation(
                sin_t[:, cols], red_s[:, :], AF.Sin),
                waits=[("ve", t_r1), ("sc", t_prev_sin_sc)], inc="sc")
            t_f2 = p.emit("v", lambda e: e.tensor_scalar(
                freq_s[:, :], freq_s[:, :], float(np.pi / 2), None, OP.add),
                waits=[("ve", t_r1)], inc="ve")
            t_k2 = p.emit("y", dma(ktab_s[:, :], k2[:, cols]),
                          waits=[("ve", t_r1)], inc="sd_ktab", inc_amt=16)
            t_r2 = cascade([("sd_ktab", t_k2), ("sc", t_sin_done)], t_f2)
            t_cos_done = p.emit("s", lambda e, cols=cols: e.activation(
                cos_t[:, cols], red_s[:, :], AF.Sin),
                waits=[("ve", t_r2)], inc="sc")
            t_prev_sin_sc = t_cos_done
            t_prev_ve = t_r2
            t_prev_k = t_r2
        t_tables = t_cos_done

        tok_stats = {}
        tok_sq = {}
        tok_xg = {}
        t_xn_grp = {}
        tok_xnw = {}
        tok_vcons = {}
        t_vevac_all = 0
        t_vevac_all_prev = 0
        kr = {}
        kv_gen = 0
        tok_wkv = {}
        tok_wkv_used = {}
        t1_of = {}
        t2_of = {}
        last_sq_g = [0]
        last_ssq_g = [0]

        def xg_load(slot, grp, sub):
            rows = slice(grp * 512, (grp + 1) * 512)
            cols = slice(sub * TW, (sub + 1) * TW)
            src = xT[rows, cols].rearrange("(c p) s -> p c s", p=128)
            dst = x_g[slot][:, :].rearrange("p (c s) -> p c s", s=TW)
            return lambda e, d=dst, s=src: e.dma_start(d, s)

        NG = 8

        def emit_xg_load(sub, pas, grp, gates):
            slot = grp % 2
            tok_xg[(sub, pas, grp)] = p.emit(
                "g", xg_load(slot, grp, sub), waits=gates,
                inc=f"sd_xg{slot}", inc_amt=16)

        def emit_stats_grp(sub, grp):
            slot = grp % 2
            wl = [(f"sd_xg{slot}", tok_xg[(sub, "A", grp)])]
            last = 0
            for c in range(4):
                cc = grp * 4 + c
                def f_sum(e, slot=slot, c=c, cc=cc):
                    return e.matmul(pl[0:1, :], ones_l[:, :],
                                    x_g[slot][:, c * TW:(c + 1) * TW],
                                    start=(cc == 0), stop=(cc == 31))
                w = list(wl) if c == 0 else []
                if cc == 0:
                    w += [("ve", t_ones_l)]
                    if sub > 0:
                        w += [("ve", t1_of[sub - 1])]
                p.emit("t", f_sum, waits=w)
                wsq = list(wl) if c == 0 else []
                if last_ssq_g[0]:
                    wsq.append(("pe", last_ssq_g[0]))
                wsq.append(("ve", last_sq_g[0]))
                last_sq_g[0] = p.emit("v", lambda e, slot=slot, c=c: e.tensor_tensor(
                    sq_s[:, :], x_g[slot][:, c * TW:(c + 1) * TW],
                    x_g[slot][:, c * TW:(c + 1) * TW], OP.mult),
                    waits=[x for x in wsq if x[1]], inc="ve")
                def f_ssq(e, cc=cc):
                    return e.matmul(prot[0:1, :], ones_l[:, :],
                                    sq_s[:, :], start=(cc == 0), stop=(cc == 31))
                w2 = [("ve", last_sq_g[0])]
                if cc == 0 and sub > 0:
                    w2.append(("ve", t_xn_grp[(sub - 1, NG - 1)]))
                last_ssq_g[0] = p.emit("t", f_ssq, waits=w2, inc="pe")
                last = last_ssq_g[0]
            tok_stats[(sub, grp)] = last
            tok_sq[(sub, grp)] = last_sq_g[0]
            if grp + 2 < NG:
                emit_xg_load(sub, "A", grp + 2,
                             [("pe", tok_stats[(sub, grp)]),
                              ("ve", tok_sq[(sub, grp)])])

        early_wq = {}
        for h0 in (0, 1):
            early_wq[h0] = p.emit("g", dma(wbig[h0][:, :], wq[h0, :, :]),
                                  inc=f"sd_wbg{h0}", inc_amt=16)
        wbk_kv = {}

        for grp in (0, 1):
            emit_xg_load(0, "A", grp, [])
        for grp in range(NG):
            emit_stats_grp(0, grp)

        bc_toks = {}
        xn_chunk_tok = {}

        def emit_rows(sub):
            t_stats = tok_stats[(sub, NG - 1)]
            t1 = p.emit("v", lambda e: e.tensor_scalar(
                mu_r[:, :], pl[0:1, :], float(1.0 / E), None, OP.mult),
                waits=[("pe", t_stats), ("sc", t_tables)], inc="ve")
            t1_of[sub] = t1
            t2 = p.emit("v", lambda e: e.tensor_scalar(
                b_r[:, :], prot[0:1, :], float(1.0 / E), None, OP.mult),
                waits=[("ve", t1)], inc="ve")
            t2_of[sub] = t2
            t3 = p.emit("v", lambda e: e.tensor_tensor(
                c_r[:, :], mu_r[:, :], mu_r[:, :], OP.mult),
                waits=[("ve", t2)], inc="ve")
            t4 = p.emit("v", lambda e: e.tensor_tensor(
                b_r[:, :], b_r[:, :], c_r[:, :], OP.subtract),
                waits=[("ve", t3)], inc="ve")
            t5 = p.emit("s", lambda e: e.activation(
                b_r[:, :], b_r[:, :], AF.Ln, bias=eps_c[0:1, 0:1]),
                waits=[("ve", t4), ("ve", t_eps), ("sc", t_tables)], inc="sc")
            t6 = p.emit("s", lambda e: e.activation(
                b_r[:, :], b_r[:, :], AF.Exp, scale=-0.5),
                waits=[("sc", t5)], inc="sc")
            t7 = p.emit("v", lambda e: e.tensor_tensor(
                c_r[:, :], mu_r[:, :], b_r[:, :], OP.mult),
                waits=[("sc", t6)], inc="ve")
            bc_toks[(sub, "t7")] = t7
            for grp in (0, 1):
                emit_xg_load(sub, "B", grp,
                             [("pe", tok_stats[(sub, NG - 2 + grp)]),
                              ("ve", tok_sq[(sub, NG - 2 + grp)])])

        def emit_bc(sub):
            t7 = bc_toks[(sub, "t7")]
            wbc = [("ve", t7), ("ve", t_ones_b)]
            if sub > 0:
                wbc.append(("ve", t_xn_grp[(sub - 1, NG - 1)]))
                wbc.append(("ve", kr[("m", sub - 1, KV - 1)]))
            bc_toks[(sub, "r")] = p.emit("t", lambda e: e.matmul(
                pq[:, :], ones_b[:, :], b_r[:, :], start=True, stop=True),
                waits=wbc, inc="pe")
            bc_toks[(sub, "mr")] = p.emit("t", lambda e: e.matmul(
                prot[:, :], ones_b[:, :], c_r[:, :], start=True, stop=True),
                waits=[("ve", t7), ("ve", t2_of[sub])], inc="pe")

        def emit_xn_grp(sub, grp, pe_gate):
            slot = grp % 2
            wl = [(f"sd_xg{slot}", tok_xg[(sub, "B", grp)]),
                  ("pe", bc_toks[(sub, "r")]), ("pe", bc_toks[(sub, "mr")]),
                  ("sc", t_tables)]
            if pe_gate:
                wl.append(("pe", pe_gate))
            prev_xn = xn_chunk_tok.get((sub, grp * 4 - 1), 0)
            for c in range(4):
                cc = grp * 4 + c
                w = list(wl) if c == 0 else []
                if prev_xn:
                    w.append(("ve", prev_xn))
                tt = p.emit("v", lambda e, slot=slot, c=c: e.tensor_tensor(
                    t_f32[:, :], x_g[slot][:, c * TW:(c + 1) * TW],
                    pq[:, :], OP.mult), waits=[x for x in w if x[1]], inc="ve")
                wx = [("ve", tt)]
                if cc == 0 and sub > 0:
                    wx.append(("sd_xnw", tok_xnw[sub - 1]))
                prev_xn = p.emit("v", lambda e, cc=cc: e.tensor_tensor(
                    xn_sub[:, cc * TW:(cc + 1) * TW], t_f32[:, :],
                    prot[:, :], OP.subtract), waits=wx, inc="ve")
                xn_chunk_tok[(sub, cc)] = prev_xn
            t_xn_grp[(sub, grp)] = prev_xn
            if grp + 2 < NG:
                emit_xg_load(sub, "B", grp + 2,
                             [("ve", t_xn_grp[(sub, grp)])])

        def emit_xnw(sub):
            def f_xnw(e, sub=sub):
                dst = xn_scr[:, :, sub * TW:(sub + 1) * TW].rearrange("c p s -> p c s")
                src = xn_sub[:, :].rearrange("p (c s) -> p c s", s=TW)
                return e.dma_start(dst, src)
            tok_xnw[sub] = p.emit("y", f_xnw,
                                  waits=[("ve", t_xn_grp[(sub, NG - 1)])],
                                  inc="sd_xnw", inc_amt=16)

        emit_rows(0)
        emit_bc(0)
        for grp in range(NG):
            emit_xn_grp(0, grp, 0)
        emit_xnw(0)

        for sub in range(4):
            cols = slice(sub * TW, (sub + 1) * TW)
            t_xn_done = t_xn_grp[(sub, NG - 1)]

            for oc in range(KV):
                slot = kv_gen % 2
                gates = []
                if kv_gen >= 2:
                    gates.append(("pe", tok_wkv_used[kv_gen - 2]))
                if kv_gen >= 2 and (kv_gen - 2) in wbk_kv:
                    gates.append(wbk_kv[kv_gen - 2])
                if sub == 0:
                    tok_wkv[kv_gen] = (f"sd_kv{slot}", p.emit(
                        "g", dma(wkv[slot][:, :], wk[oc, :, :]),
                        waits=gates, inc=f"sd_kv{slot}", inc_amt=16))
                    wbk_kv[kv_gen] = (f"sd_wkbk{slot}", p.emit(
                        "y", dma(wk_bf[oc, :, :], wkv[slot][:, :]),
                        waits=[tok_wkv[kv_gen]], inc=f"sd_wkbk{slot}", inc_amt=16))
                    wbk_kv[("k", oc)] = wbk_kv[kv_gen]
                else:
                    gates.append(wbk_kv[("k", oc)])
                    tok_wkv[kv_gen] = (f"sd_kvh{slot}", p.emit(
                        "y", dma(wkv[slot][:, :], wk_bf[oc, :, :]),
                        waits=gates, inc=f"sd_kvh{slot}", inc_amt=16))
                this_gen = kv_gen
                kv_gen += 1
                kp = pattn[:, :] if oc % 2 == 0 else psc1[:, 0:512]
                last = 0
                for c in range(NCHUNK):
                    w = []
                    if oc == 0:
                        w.append(("ve", xn_chunk_tok[(sub, c)]))
                    if c == 0:
                        w += [tok_wkv[this_gen]]
                        if oc >= 2:
                            w.append(("sc", kr[("sc", sub, oc - 2)]))
                        elif sub > 0:
                            w.append(("sc", kr[("sc", sub - 1, KV - 2 + oc)]))
                            w.append(("sc", t_vevac_all_prev))
                        if oc == 1:
                            w.append(("ve", t_xn_done))
                    last = p.emit("t", lambda e, slot=slot, c=c, kp=kp: e.matmul(
                        kp, wkv[slot][:, c * 128:(c + 1) * 128],
                        xn_sub[:, c * TW:(c + 1) * TW],
                        start=(c == 0), stop=(c == NCHUNK - 1)),
                        waits=w, inc="pe" if c == NCHUNK - 1 else None)
                tok_wkv_used[this_gen] = last
                prev_rot = kr.get(("rot", sub, oc - 1), kr.get(("rot", sub - 1, KV - 1), 0))
                prev_c1 = kr.get(("c1", sub, oc - 1), kr.get(("c1", sub - 1, KV - 1), 0))
                prev_m = kr.get(("m", sub, oc - 1), kr.get(("m", sub - 1, KV - 1), 0))
                prev_add = kr.get(("add", sub, oc - 1), kr.get(("add", sub - 1, KV - 1), 0))
                kr[("sc", sub, oc)] = p.emit(
                    "s", lambda e, kp=kp: e.activation(k_tmp[:, :], kp, AF.Copy),
                    waits=[x for x in [("pe", last), ("pe", prev_rot), ("ve", prev_c1)] if x[1]],
                    inc="sc")
                w_rot = [("sc", kr[("sc", sub, oc)]), ("sd_const", t_c), ("ve", prev_m)]
                kr[("rot", sub, oc)] = p.emit(
                    "t", lambda e: e.matmul(pq[:, :], rt_sb[:, :], k_tmp[:, :],
                                            start=True, stop=True),
                    waits=[x for x in w_rot if x[1]], inc="pe")
                kr[("c1", sub, oc)] = p.emit(
                    "v", lambda e, cols=cols: e.tensor_tensor(
                        kc_f32[:, :], k_tmp[:, :], cos_t[:, cols], OP.mult),
                    waits=[x for x in [("sc", kr[("sc", sub, oc)]), ("sc", t_tables),
                                       ("ve", prev_add)] if x[1]], inc="ve")
                kr[("m", sub, oc)] = p.emit(
                    "v", lambda e, cols=cols: e.tensor_tensor(
                        km_f32[:, :], pq[:, :], sin_t[:, cols], OP.mult),
                    waits=[("pe", kr[("rot", sub, oc)]), ("ve", kr[("c1", sub, oc)])],
                    inc="ve")
                kr[("add", sub, oc)] = p.emit(
                    "v", lambda e, oc=oc, sub=sub: e.tensor_tensor(
                        K_sb[:, oc * S + sub * TW: oc * S + (sub + 1) * TW],
                        kc_f32[:, :], km_f32[:, :], OP.add),
                    waits=[("ve", kr[("m", sub, oc)])], inc="ve")
                if sub + 1 < 4:
                    if oc == 0:
                        for grp in (0, 1):
                            emit_xg_load(sub + 1, "A", grp,
                                         [("ve", t_xn_grp[(sub, NG - 2 + grp)])])
                    if oc % 2 == 1:
                        emit_stats_grp(sub + 1, oc // 2)

            vps = [psc0[:, 0:512], psc0[:, 512:1024], psc1[:, 0:512], psc1[:, 512:1024]]
            t_vev = {}
            for half in range(2):
                wv_tok = {}
                wv_slot = {}
                wv_genid = {}

                def emit_wv_load(wg, half):
                    nonlocal kv_gen
                    slot = kv_gen % 2
                    gates = []
                    if kv_gen >= 2:
                        gates.append(("pe", tok_wkv_used[kv_gen - 2]))
                    if kv_gen >= 2 and (kv_gen - 2) in wbk_kv:
                        gates.append(wbk_kv[kv_gen - 2])
                    if sub == 0:
                        def f_wv(e, wg=wg, half=half, slot=slot):
                            src = wvt[wg * 8:(wg + 1) * 8, :, half * 512:(half + 1) * 512]
                            dst = wkv[slot][:, :].rearrange("p (c s) -> p c s", s=512)
                            return e.dma_start(dst, src.rearrange("c p s -> p c s"))
                        wv_tok[wg] = (f"sd_kv{slot}", p.emit(
                            "g", f_wv, waits=gates, inc=f"sd_kv{slot}", inc_amt=16))
                        def f_wvb(e, wg=wg, half=half, slot=slot):
                            dst = wv_bf[wg * 8:(wg + 1) * 8, :, half * 512:(half + 1) * 512]
                            src2 = wkv[slot][:, :].rearrange("p (c s) -> p c s", s=512)
                            return e.dma_start(dst.rearrange("c p s -> p c s"), src2)
                        wbk_kv[kv_gen] = (f"sd_wkbk{slot}", p.emit(
                            "y", f_wvb, waits=[wv_tok[wg]],
                            inc=f"sd_wkbk{slot}", inc_amt=16))
                        wbk_kv[("v", wg, half)] = wbk_kv[kv_gen]
                    else:
                        gates.append(wbk_kv[("v", wg, half)])
                        def f_wv(e, wg=wg, half=half, slot=slot):
                            src = wv_bf[wg * 8:(wg + 1) * 8, :, half * 512:(half + 1) * 512]
                            dst = wkv[slot][:, :].rearrange("p (c s) -> p c s", s=512)
                            return e.dma_start(dst, src.rearrange("c p s -> p c s"))
                        wv_tok[wg] = (f"sd_kvh{slot}", p.emit(
                            "y", f_wv, waits=gates, inc=f"sd_kvh{slot}", inc_amt=16))
                    wv_slot[wg] = slot
                    wv_genid[wg] = kv_gen
                    kv_gen += 1

                for wg in (0, 1):
                    emit_wv_load(wg, half)
                last_mm = 0
                for wg in range(4):
                    slot = wv_slot[wg]
                    for c8 in range(8):
                        c = wg * 8 + c8
                        for ts in range(4):
                            w = []
                            if c8 == 0 and ts == 0:
                                w = [wv_tok[wg]]
                                if c == 0:
                                    if half == 0:
                                        w += [("ve", t2_of[sub]), ("sc", kr[("sc", sub, KV - 1)]),
                                              ("ve", t_xn_done)]
                                    else:
                                        w += [("sc", t_vev[(0, 3)])]
                            inc = "pe" if (c8 == 7 and ts == 3) else None
                            r = p.emit("t", lambda e, slot=slot, c8=c8, c=c, ts=ts, vp=vps[ts]: e.matmul(
                                vp, xn_sub[:, c * TW + ts * 128: c * TW + (ts + 1) * 128],
                                wkv[slot][:, c8 * 512:(c8 + 1) * 512],
                                start=(c == 0), stop=(c == NCHUNK - 1)),
                                waits=w, inc=inc)
                            if r:
                                last_mm = r
                    tok_wkv_used[wv_genid[wg]] = last_mm
                    if wg + 2 < 4:
                        emit_wv_load(wg + 2, half)
                    if sub + 1 < 4:
                        if half == 0:
                            emit_stats_grp(sub + 1, 4 + wg)
                            if wg == 3:
                                emit_rows(sub + 1)
                        else:
                            if wg == 0:
                                emit_bc(sub + 1)
                            emit_xn_grp(sub + 1, 2 * wg, last_mm)
                            emit_xn_grp(sub + 1, 2 * wg + 1, 0)
                for ts in range(4):
                    kt_glob = sub * 4 + ts
                    t_vev[(half, ts)] = p.emit(
                        "s", lambda e, vp=vps[ts], half=half, kt_glob=kt_glob: e.activation(
                            V_sb[:, kt_glob * 1024 + half * 512: kt_glob * 1024 + (half + 1) * 512],
                            vp, AF.Copy),
                        waits=[("pe", last_mm)], inc="sc")
            t_vevac_all = t_vev[(1, 3)]
            t_vevac_all_prev = t_vevac_all
            tok_vcons[sub] = last_mm
            if sub + 1 < 4:
                emit_xnw(sub + 1)

        w_lora_gate = [("pe", tok_wkv_used[kv_gen - 1]), ("pe", tok_wkv_used[kv_gen - 2])]
        p.emit("g", dma(wkv[0][:, :], wa[:, :]), waits=w_lora_gate,
               inc="sd_lora", inc_amt=16)
        t_lora = p.emit("g", dma(wkv[1][:, :], wb[:, :]), inc="sd_lora", inc_amt=16)
        wa_s, wb_s = wkv[0], wkv[1]

        gh_q = {}
        tok_qevac = {}
        tok_rot = {}
        tok_c1 = {}
        tok_m = {}
        tok_add = {}
        tok_scores = {}
        tok_pv = {}
        tok_l = {}
        tok_recip = {}
        tok_bc = {}
        tok_bcevac = {}
        tok_attnmul = {}
        pair_exp = {}
        tok_out = {}
        tok_stt = {}
        tok_store = {}
        tok_xres = {}
        tok_wbig = {}
        wbk_q = {}
        wbk_ow = {}
        tok_xn_load = {}
        state = {"pair": 0}
        prev_use_slot = {0: 0, 1: 0}
        prev_wbk_slot = {0: 0, 1: 0}
        prev_out_last = 0
        t_hs = 0
        t_hs_prev = 0

        def wq_load(qst, h):
            slot = h % 2
            if qst == 0 and h in (0, 1):
                tok_wbig[(qst, "q", h)] = (f"sd_wbg{slot}", early_wq[h])
                return
            gates = []
            if prev_use_slot[slot]:
                gates.append(("pe", prev_use_slot[slot]))
            if prev_wbk_slot[slot]:
                gates.append((f"sd_wbk{slot}", prev_wbk_slot[slot]))
            if qst == 0:
                sem = f"sd_wbg{slot}"
                tok = p.emit("g", dma(wbig[slot][:, :], wq[h, :, :]),
                             waits=gates, inc=sem, inc_amt=16)
            else:
                sem = f"sd_wb{slot}"
                gates.append((f"sd_wbk{h % 2}", wbk_q[h]))
                tok = p.emit("y", dma(wbig[slot][:, :], wq_bf[h, :, :]),
                             waits=gates, inc=sem, inc_amt=16)
            tok_wbig[(qst, "q", h)] = (sem, tok)

        def wow_load(qst, oc, half):
            u = 2 * oc + half
            slot = u % 2
            gates = []
            if prev_use_slot[slot]:
                gates.append(("pe", prev_use_slot[slot]))
            if prev_wbk_slot[slot]:
                gates.append((f"sd_wbk{slot}", prev_wbk_slot[slot]))
            if qst == 0:
                sem = f"sd_wbg{slot}"
                src = wo[oc, :, :] if half == 0 else wf[oc, :, :]
                tok = p.emit("g", dma(wbig[slot][:, :], src),
                             waits=gates, inc=sem, inc_amt=16)
            else:
                sem = f"sd_wb{slot}"
                gates.append((f"sd_wbk{slot}", wbk_ow[u]))
                tok = p.emit("y", dma(wbig[slot][:, :],
                                      wow_bf[oc, :, half * E:(half + 1) * E]),
                             waits=gates, inc=sem, inc_amt=16)
            tok_wbig[(qst, "ow", u)] = (sem, tok)

        def xres_load(qst, oc):
            gates = []
            if (qst, oc - 1) in tok_stt:
                gates.append(("ve", tok_stt[(qst, oc - 1)]))
            elif qst > 0:
                gates.append(("ve", tok_stt[(qst - 1, NCHUNK - 1)]))
            tok_xres[(qst, oc)] = p.emit(
                "y", lambda e, qst=qst, oc=oc: e.dma_start(
                    x_res[:, :],
                    xT[oc * 128:(oc + 1) * 128, qst * TW:(qst + 1) * TW]),
                waits=gates, inc="sd_xres", inc_amt=16)

        def head_tail(qst, h, fill=None):
            gh = qst * H + h
            if fill is None:
                fill = lambda n: None
            kts = KT_SCHED[qst]
            nk = len(kts)
            npair = nk // 2
            pb = state["pair"]
            kv = h // 4
            t_exp = {}
            t_mask = {}
            pv_done = {}

            def emit_lpv(j):
                slot = (pb + j) % 2
                last_l = last_pv = 0
                for half in range(2):
                    pos = 2 * j + half
                    kt = kts[pos][0]
                    sl = slice(slot * 1024 + half * 512, slot * 1024 + (half + 1) * 512)
                    w = []
                    if half == 0:
                        w = [("sc", t_exp[j]), ("ve", t_ones_l)]
                        if t_mask.get(j):
                            w.append(("ve", t_mask[j]))
                        if j == 0:
                            if gh > 0:
                                w.append(("ve", tok_bcevac[gh - 1]))
                            else:
                                w.append(("ve", kr[("add", 3, KV - 1)]))
                            if h == 0 and qst > 0:
                                w.append(("ve", t_hs_prev))
                    last_l = p.emit("t", lambda e, sl=sl, pos=pos: e.matmul(
                        pl[0:1, :], ones_l[:, :], p_sb[:, sl],
                        start=(pos == 0), stop=(pos == nk - 1)),
                        waits=w, inc="pe" if pos == nk - 1 else None)
                    w2 = []
                    if half == 0 and j == 0:
                        w2 = [("sc", t_exp[j])]
                        if t_mask.get(j):
                            w2.append(("ve", t_mask[j]))
                        if gh > 0:
                            w2.append(("ve", tok_attnmul[gh - 1]))
                        else:
                            w2.append(("sc", kr[("sc", 3, KV - 1)]))
                    last_pv = p.emit("t", lambda e, sl=sl, pos=pos, kt=kt, kv=kv: e.matmul(
                        pattn[:, :], V_sb[:, kt * 1024 + kv * 128: kt * 1024 + (kv + 1) * 128],
                        p_sb[:, sl], start=(pos == 0), stop=(pos == nk - 1)),
                        waits=w2, inc="pe" if half == 1 else None)
                pv_done[j] = last_pv
                return last_l, last_pv

            last_l = last_pv = 0
            for j in range(npair):
                fill(4)
                pidx = pb + j
                par = pidx % 2
                slot = pidx % 2
                t_mm = 0
                for half in range(2):
                    kt = kts[2 * j + half][0]
                    w = []
                    if half == 0:
                        w = [("ve", tok_add[gh])]
                        if pidx >= 2:
                            w.append(("sc", pair_exp[pidx - 2]))
                        else:
                            w.append(("sc", t_vevac_all))
                    t_mm = p.emit("t", lambda e, par=par, half=half, kt=kt, kv=kv: e.matmul(
                        psc[par][:, half * 512:(half + 1) * 512],
                        K_sb[:, kv * S + kt * 128: kv * S + (kt + 1) * 128],
                        q_rope[:, :], start=True, stop=True),
                        waits=w, inc="pe" if half == 1 else None)
                tok_scores[gh] = t_mm
                w_e = [("pe", t_mm)]
                if j >= 2:
                    w_e.append(("pe", pv_done[j - 2]))
                elif gh > 0:
                    w_e.append(("pe", tok_pv[gh - 1]))
                t_exp[j] = p.emit("s", lambda e, par=par, slot=slot: e.activation(
                    p_sb[:, slot * 1024:(slot + 1) * 1024], psc[par][:, :], AF.Exp),
                    waits=w_e, inc="sc")
                pair_exp[pidx] = t_exp[j]
                tm = 0
                for half in range(2):
                    pos = 2 * j + half
                    mi = kts[pos][1]
                    if mi is None:
                        continue
                    sl = slice(slot * 1024 + half * 512, slot * 1024 + (half + 1) * 512)
                    w = [("sc", t_exp[j]), ("sd_const", t_c)]
                    if tm:
                        w.append(("ve", tm))
                    tm = p.emit("v", lambda e, sl=sl, mi=mi: e.tensor_tensor(
                        p_sb[:, sl], p_sb[:, sl],
                        masks_sb[:, mi * 512:(mi + 1) * 512], OP.mult),
                        waits=w, inc="ve")
                t_mask[j] = tm
                if j >= 1:
                    fill(4)
                    last_l, last_pv = emit_lpv(j - 1)
            fill(4)
            last_l, last_pv = emit_lpv(npair - 1)
            state["pair"] = pb + npair
            tok_l[gh] = last_l
            tok_pv[gh] = last_pv
            fill(8)
            tok_recip[gh] = p.emit("v", lambda e: e.reciprocal(linv_r[:, :], pl[0:1, :]),
                                   waits=[("pe", last_l)], inc="ve")
            tok_bc[gh] = p.emit("t", lambda e: e.matmul(
                pl[:, :], ones_b[:, :], linv_r[:, :], start=True, stop=True),
                waits=[("ve", tok_recip[gh]), ("ve", t_ones_b)], inc="pe")
            tok_bcevac[gh] = p.emit("v", lambda e: e.tensor_copy(linv_b[:, :], pl[:, :]),
                                    waits=[("pe", tok_bc[gh])], inc="ve")
            tok_attnmul[gh] = p.emit("v", lambda e, h=h: e.tensor_tensor(
                attn_t[:, h * TW:(h + 1) * TW], pattn[:, :], linv_b[:, :], OP.mult),
                waits=[("pe", last_pv), ("ve", tok_bcevac[gh])], inc="ve")

        for qst in range(NQST):
            cols = slice(qst * TW, (qst + 1) * TW)
            w_xn = [("sd_xnw", tok_xnw[3])]
            if qst == 0:
                w_xn.append(("pe", tok_vcons[3]))
            else:
                w_xn.append(("pe", prev_out_last))
            tok_xn_half = p.emit(
                "y", lambda e, qst=qst: e.dma_start(
                    xn_sub[:, 0:16 * TW].rearrange("p (c s) -> p c s", s=TW),
                    xn_scr[0:16, :, qst * TW:(qst + 1) * TW].rearrange("c p s -> p c s")),
                waits=w_xn, inc="sd_xnr", inc_amt=16)
            tok_xn_load[qst] = p.emit(
                "y", lambda e, qst=qst: e.dma_start(
                    xn_sub[:, 16 * TW:].rearrange("p (c s) -> p c s", s=TW),
                    xn_scr[16:32, :, qst * TW:(qst + 1) * TW].rearrange("c p s -> p c s")),
                inc="sd_xnr2", inc_amt=16)

            for h in (0, 1):
                wq_load(qst, h)

            for h in range(H):
                gh = qst * H + h
                slot = h % 2
                qstate = {"c": 0, "last": 0}

                def fill_q(n, h=h, gh=gh, slot=slot, qstate=qstate, qst=qst,
                           txh=tok_xn_half):
                    while n > 0 and qstate["c"] < NCHUNK:
                        c = qstate["c"]
                        w = []
                        if c == 0:
                            w = [tok_wbig[(qst, "q", h)], ("sd_xnr", txh)]
                            if gh > 0:
                                w.append(("sc", tok_qevac[gh - 1]))
                            if h == 0:
                                if qst == 0:
                                    w.append(("ve", kr[("m", 3, KV - 1)]))
                                else:
                                    w.append(("ve", tok_stt[(qst - 1, NCHUNK - 1)]))
                        elif c == 16:
                            w = [("sd_xnr2", tok_xn_load[qst])]
                        qstate["last"] = p.emit("t", lambda e, slot=slot, c=c: e.matmul(
                            pq[:, :], wbig[slot][:, c * 128:(c + 1) * 128],
                            xn_sub[:, c * TW:(c + 1) * TW],
                            start=(c == 0), stop=(c == NCHUNK - 1)),
                            waits=w, inc="pe" if c == NCHUNK - 1 else None)
                        qstate["c"] += 1
                        n -= 1

                if h > 0:
                    head_tail(qst, h - 1, fill_q)
                fill_q(NCHUNK)
                gh_q[gh] = qstate["last"]
                prev_use_slot[slot] = qstate["last"]

                if qst == 0:
                    wbk_q[h] = p.emit(
                        "y", dma(wq_bf[h, :, :], wbig[slot][:, :]),
                        waits=[tok_wbig[(qst, "q", h)]],
                        inc=f"sd_wbk{slot}", inc_amt=16)
                    prev_wbk_slot[slot] = wbk_q[h]
                if h + 2 < H:
                    wq_load(qst, h + 2)

                w_ev = [("pe", gh_q[gh])]
                if gh > 0:
                    w_ev += [("pe", tok_rot[gh - 1]), ("ve", tok_c1[gh - 1])]
                tok_qevac[gh] = p.emit("s", lambda e: e.activation(
                    q_sb[:, :], pq[:, :], AF.Copy, scale=INV_SQRT_D),
                    waits=w_ev, inc="sc")

                w_rot = [("sc", tok_qevac[gh])]
                if gh > 0:
                    w_rot.append(("ve", tok_m[gh - 1]))
                else:
                    w_rot.append(("ve", kr[("m", 3, KV - 1)]))
                if h == 0 and qst > 0:
                    w_rot.append(("ve", tok_stt[(qst - 1, NCHUNK - 1)]))
                tok_rot[gh] = p.emit("t", lambda e: e.matmul(
                    prot[:, :], rt_sb[:, :], q_sb[:, :], start=True, stop=True),
                    waits=w_rot, inc="pe")
                w_c1 = [("sc", tok_qevac[gh])]
                if gh > 0:
                    w_c1.append(("ve", tok_add[gh - 1]))
                else:
                    w_c1.append(("ve", kr[("add", 3, KV - 1)]))
                tok_c1[gh] = p.emit("v", lambda e, cols=cols: e.tensor_tensor(
                    kc_f32[:, :], q_sb[:, :], cos_t[:, cols], OP.mult),
                    waits=w_c1, inc="ve")
                tok_m[gh] = p.emit("v", lambda e, cols=cols: e.tensor_tensor(
                    km_f32[:, :], prot[:, :], sin_t[:, cols], OP.mult),
                    waits=[("pe", tok_rot[gh]), ("ve", tok_c1[gh])], inc="ve")
                w_add = [("ve", tok_m[gh])]
                if gh > 0:
                    w_add.append(("pe", tok_scores[gh - 1]))
                tok_add[gh] = p.emit("v", lambda e: e.tensor_tensor(
                    q_rope[:, :], kc_f32[:, :], km_f32[:, :], OP.add),
                    waits=w_add, inc="ve")

            head_tail(qst, H - 1)
            gh_last = qst * H + H - 1

            w_h = [("ve", tok_bcevac[gh_last]), ("sd_lora", t_lora),
                   ("sd_xnr", tok_xn_half), ("sd_xnr2", tok_xn_load[qst])]
            last = 0
            for c in range(NCHUNK):
                last = p.emit("t", lambda e, c=c: e.matmul(
                    pl[:, :], wa_s[:, c * 128:(c + 1) * 128],
                    xn_sub[:, c * TW:(c + 1) * TW],
                    start=(c == 0), stop=(c == NCHUNK - 1)),
                    waits=w_h if c == 0 else [], inc="pe" if c == NCHUNK - 1 else None)
            w_hs = [("pe", last), ("sd_const", t_c)]
            if qst > 0:
                w_hs.append(("pe", tok_out[(qst - 1, NCHUNK - 1)]))
            t_hs = p.emit("v", lambda e: e.tensor_scalar(
                h_sb[:, :], pl[:, :], coef_sb[:, 0:1], 2.0, OP.mult, OP.mult),
                waits=w_hs, inc="ve")
            t_hs_prev = t_hs

            for u in (0, 1):
                wow_load(qst, u // 2, u % 2)
            xres_load(qst, 0)

            for oc in range(NCHUNK):
                pout = pq if oc % 2 == 0 else prot
                u0, u1 = 2 * oc, 2 * oc + 1
                s0, s1 = u0 % 2, u1 % 2
                w0 = [tok_wbig[(qst, "ow", u0)],
                      ("ve", tok_attnmul[gh_last]), ("ve", t_hs)]
                if (qst, oc - 2) in tok_stt:
                    w0.append(("ve", tok_stt[(qst, oc - 2)]))
                else:
                    w0.append(("sc", tok_qevac[gh_last]))
                    w0.append(("ve", tok_m[gh_last]))
                last = 0
                for h in range(H):
                    last = p.emit("t", lambda e, s0=s0, h=h, pout=pout: e.matmul(
                        pout[:, :], wbig[s0][:, h * 128:(h + 1) * 128],
                        attn_t[:, h * TW:(h + 1) * TW],
                        start=(h == 0), stop=False),
                        waits=w0 if h == 0 else [], inc="pe" if h == H - 1 else None)
                prev_use_slot[s0] = last
                if qst == 0:
                    wbk_ow[u0] = p.emit(
                        "y", dma(wow_bf[oc, :, 0:E], wbig[s0][:, :]),
                        waits=[tok_wbig[(qst, "ow", u0)]],
                        inc=f"sd_wbk{s0}", inc_amt=16)
                    prev_wbk_slot[s0] = wbk_ow[u0]
                if u0 + 2 < 2 * NCHUNK:
                    wow_load(qst, (u0 + 2) // 2, (u0 + 2) % 2)

                w1 = [tok_wbig[(qst, "ow", u1)]]
                lastw = 0
                for c in range(NCHUNK):
                    lastw = p.emit("t", lambda e, s1=s1, c=c, pout=pout: e.matmul(
                        pout[:, :], wbig[s1][:, c * 128:(c + 1) * 128],
                        xn_sub[:, c * TW:(c + 1) * TW],
                        start=False, stop=False),
                        waits=w1 if c == 0 else [], inc="pe" if c == NCHUNK - 1 else None)
                prev_use_slot[s1] = lastw
                if qst == 0:
                    wbk_ow[u1] = p.emit(
                        "y", dma(wow_bf[oc, :, E:2 * E], wbig[s1][:, :]),
                        waits=[tok_wbig[(qst, "ow", u1)]],
                        inc=f"sd_wbk{s1}", inc_amt=16)
                    prev_wbk_slot[s1] = wbk_ow[u1]
                if u1 + 2 < 2 * NCHUNK:
                    wow_load(qst, (u1 + 2) // 2, (u1 + 2) % 2)

                tok_out[(qst, oc)] = p.emit("t", lambda e, oc=oc, pout=pout: e.matmul(
                    pout[:, :], wb_s[:, oc * 128:(oc + 1) * 128],
                    h_sb[:, :], start=False, stop=True),
                    waits=[("sd_lora", t_lora), ("ve", t_hs)], inc="pe")

                w_stt = [("pe", tok_out[(qst, oc)]),
                         ("sd_xres", tok_xres[(qst, oc)]),
                         ("sd_const", t_c)]
                if (qst, oc - 1) in tok_store:
                    w_stt.append(("sd_out", tok_store[(qst, oc - 1)]))
                elif qst > 0:
                    w_stt.append(("sd_out", tok_store[(qst - 1, NCHUNK - 1)]))
                tok_stt[(qst, oc)] = p.emit(
                    "v", lambda e, oc=oc, pout=pout: e.scalar_tensor_tensor(
                        out_sb[:, :], pout[:, :], bias_sb[:, oc:oc + 1],
                        x_res[:, :], OP.add, OP.add),
                    waits=w_stt, inc="ve")
                tok_store[(qst, oc)] = p.emit(
                    "y", lambda e, oc=oc, qst=qst: e.dma_start(
                        outT[oc * 128:(oc + 1) * 128, qst * TW:(qst + 1) * TW],
                        out_sb[:, :]),
                    waits=[("ve", tok_stt[(qst, oc)])], inc="sd_out", inc_amt=16)
                if oc + 1 < NCHUNK:
                    xres_load(qst, oc + 1)
            prev_out_last = tok_out[(qst, NCHUNK - 1)]

        p.emit("y", lambda e: e.nop(),
               waits=[("sd_out", tok_store[(NQST - 1, NCHUNK - 1)])])

        sem_stack = contextlib.ExitStack()
        with sem_stack:
            sems = {}
            for name in list(p.count):
                sems[name] = sem_stack.enter_context(nc.semaphore(name))
            with nc.Block() as block:
                block.tensor(p.runner("t", sems))
                block.vector(p.runner("v", sems))
                block.scalar(p.runner("s", sems))
                block.gpsimd(p.runner("g", sems))
                block.sync(p.runner("y", sems))
    return nc



def _prep_inputs(L0_attn_hidden_states, L0_attn_Wq, L0_attn_Wk, L0_attn_Wv,
                 L0_attn_Wo, L0_ffn_lora_A, L0_ffn_lora_B, L0_ffn_W,
                 L0_ffn_bias, L0_ffn_lora_weights):
    f32 = np.float32
    x = np.asarray(L0_attn_hidden_states)
    Wq = np.asarray(L0_attn_Wq); Wk = np.asarray(L0_attn_Wk)
    Wv = np.asarray(L0_attn_Wv); Wo = np.asarray(L0_attn_Wo)
    A = np.asarray(L0_ffn_lora_A); B = np.asarray(L0_ffn_lora_B)
    W = np.asarray(L0_ffn_W); bias = np.asarray(L0_ffn_bias)
    lw = np.asarray(L0_ffn_lora_weights)

    com = {}
    com["wq"] = np.ascontiguousarray(Wq.reshape(H, 128, NCHUNK, 128).transpose(0, 3, 2, 1)).reshape(H, 128, E)
    com["wk"] = np.ascontiguousarray(Wk.reshape(KV, 128, NCHUNK, 128).transpose(0, 3, 2, 1)).reshape(KV, 128, E)
    com["wvt"] = np.ascontiguousarray(Wv.T).reshape(NCHUNK, 128, 1024)
    com["wo"] = np.ascontiguousarray(Wo.reshape(NCHUNK, 128, H, 128).transpose(0, 3, 2, 1)).reshape(NCHUNK, 128, E)
    com["wf"] = np.ascontiguousarray(W.reshape(NCHUNK, 128, NCHUNK, 128).transpose(0, 3, 2, 1)).reshape(NCHUNK, 128, E)
    A_all = A.reshape(128, E)
    com["wa"] = np.ascontiguousarray(
        A_all.T.reshape(NCHUNK, 128, 128).transpose(1, 0, 2)).reshape(128, E)
    com["wb"] = np.ascontiguousarray(B.transpose(0, 2, 1).reshape(128, E))
    com["bias_t"] = np.ascontiguousarray(bias.reshape(NCHUNK, 128).T)
    com["coef"] = np.ascontiguousarray(np.repeat(lw, 16).reshape(128, 1))

    R = np.zeros((128, 128), dtype=f32)
    for i in range(64):
        R[i, i + 64] = -1.0
        R[i + 64, i] = 1.0
    com["rt"] = np.ascontiguousarray(R.T.astype(ml_dtypes.bfloat16))

    kl = np.arange(128)[:, None]
    ql = np.arange(TW)[None, :]
    m = np.zeros((8, 128, TW), dtype=f32)
    for j in range(4):
        m[j] = (ql <= kl + j * 128)
        m[4 + j] = (ql >= kl + j * 128)
    com["masks"] = np.ascontiguousarray(
        m.transpose(1, 0, 2).reshape(128, 8 * TW).astype(np.uint8))

    d = np.arange(128, dtype=np.float64) % 64
    invfreq = 10000.0 ** (-d / 64.0)
    t = np.arange(S, dtype=np.float64)
    freqs = invfreq[:, None] * t[None, :]
    com["k1"] = np.ascontiguousarray(np.round(freqs / (2 * np.pi)).astype(f32))
    com["k2"] = np.ascontiguousarray(np.round((freqs + np.pi / 2) / (2 * np.pi)).astype(f32))

    per_core = [np.ascontiguousarray(x[b].T) for b in range(x.shape[0])]
    return com, per_core


_NC_CACHE = {}


def _get_nc():
    if "nc" not in _NC_CACHE:
        _NC_CACHE["nc"] = build_program()
    return _NC_CACHE["nc"]


def kernel(**inputs):
    com, per_core = _prep_inputs(**inputs)
    nc = _get_nc()
    in_maps = []
    for b in range(len(per_core)):
        m = dict(com)
        m["xT"] = per_core[b]
        in_maps.append(m)
    res = run_bass_kernel_spmd(nc, in_maps, core_ids=list(range(len(per_core))))
    outs = [np.ascontiguousarray(np.asarray(r["outT"]).T.astype(np.float32))
            for r in res.results]
    return np.stack(outs, axis=0)


# version 07
# speedup vs baseline: 1.0515x; 1.0172x over previous
import contextlib
import numpy as np
import ml_dtypes

import concourse.bass as bass
import concourse.mybir as mybir
from concourse.alu_op_type import AluOpType as OP
from concourse.bass_utils import run_bass_kernel_spmd

F32 = mybir.dt.float32
BF16 = mybir.dt.bfloat16
AF = mybir.ActivationFunctionType

S = 2048
E = 4096
NCHUNK = 32
H = 32
KV = 8
NQST = 4
TW = 512
EPS = 1e-5
INV_SQRT_D = float(1.0 / np.sqrt(128.0))

CW1 = float(np.float32(2.0 * np.pi))
CW2 = float(np.float32(2.0 * np.pi - np.float64(CW1)))
CW3 = float(np.float32(2.0 * np.pi - np.float64(CW1) - np.float64(CW2)))
PI_F = float(np.float32(np.pi))

KT_SCHED = {
    0: [(8 + j, j) for j in range(4)] + [(12 + j, None) for j in range(4)],
    1: [(12 + j, j) for j in range(4)],
    2: [(j, 4 + j) for j in range(4)],
    3: [(j, None) for j in range(4)] + [(4 + j, 4 + j) for j in range(4)],
}


class Plan:
    def __init__(self):
        self.ops = []
        self.count = {}

    def emit(self, eng, fn, waits=(), inc=None, inc_amt=1):
        w = tuple((s, v) for (s, v) in waits if s is not None and v)
        tok = None
        if inc is not None:
            self.count[inc] = self.count.get(inc, 0) + inc_amt
            tok = self.count[inc]
        self.ops.append((eng, w, fn, inc, inc_amt))
        return tok

    def runner(self, name, sems):
        ops = self.ops

        def run(eng):
            hw = {}
            for (engine, waits, fn, inc, inc_amt) in ops:
                if engine != name:
                    continue
                for (sname, v) in waits:
                    if hw.get(sname, 0) >= v:
                        continue
                    eng.wait_ge(sems[sname], v)
                    hw[sname] = v
                inst = fn(eng)
                if inc is not None:
                    inst.then_inc(sems[inc], inc_amt)
        return run


def build_program():
    nc = bass.Bass("TRN2", target_bir_lowering=False)
    p = Plan()

    xT = nc.dram_tensor("xT", [E, S], F32, kind="ExternalInput")
    wq = nc.dram_tensor("wq", [H, 128, E], F32, kind="ExternalInput")
    wk = nc.dram_tensor("wk", [KV, 128, E], F32, kind="ExternalInput")
    wvt = nc.dram_tensor("wvt", [NCHUNK, 128, 1024], F32, kind="ExternalInput")
    wo = nc.dram_tensor("wo", [NCHUNK, 128, E], F32, kind="ExternalInput")
    wf = nc.dram_tensor("wf", [NCHUNK, 128, E], F32, kind="ExternalInput")
    wa = nc.dram_tensor("wa", [128, E], F32, kind="ExternalInput")
    wb = nc.dram_tensor("wb", [128, E], F32, kind="ExternalInput")
    bias_t = nc.dram_tensor("bias_t", [128, NCHUNK], F32, kind="ExternalInput")
    coef = nc.dram_tensor("coef", [128, 1], F32, kind="ExternalInput")
    rt = nc.dram_tensor("rt", [128, 128], BF16, kind="ExternalInput")
    masks = nc.dram_tensor("masks", [128, 8 * TW], mybir.dt.uint8, kind="ExternalInput")
    k1 = nc.dram_tensor("k1", [128, S], F32, kind="ExternalInput")
    k2 = nc.dram_tensor("k2", [128, S], F32, kind="ExternalInput")
    outT = nc.dram_tensor("outT", [E, S], F32, kind="ExternalOutput")
    wq_bf = nc.dram_tensor("wq_bf", [H, 128, E], BF16)
    wk_bf = nc.dram_tensor("wk_bf", [KV, 128, E], BF16)
    wv_bf = nc.dram_tensor("wv_bf", [4, 2, 128, E], BF16)
    wow_bf = nc.dram_tensor("wow_bf", [NCHUNK, 128, 2 * E], BF16)
    xn_scr = nc.dram_tensor("xn_scr", [4, 128, NCHUNK * TW], BF16)

    es = contextlib.ExitStack()
    with es:
        sbt = lambda name, shape, dt: es.enter_context(nc.sbuf_tensor(name, shape, dt))
        cos_t = sbt("cos_t", [128, S], F32)
        sin_t = sbt("sin_t", [128, S], F32)
        rt_sb = sbt("rt_sb", [128, 128], BF16)
        ones_l = sbt("ones_l", [128, 1], BF16)
        ones_b = sbt("ones_b", [1, 128], F32)
        masks_sb = sbt("masks_sb", [128, 8 * TW], mybir.dt.uint8)
        bias_sb = sbt("bias_sb", [128, NCHUNK], F32)
        coef_sb = sbt("coef_sb", [128, 1], F32)
        K_sb = sbt("K_sb", [128, KV * S], BF16)
        V_sb = sbt("V_sb", [128, 16 * 1024], BF16)
        wbig = [sbt(f"wbig{i}", [128, E], BF16) for i in range(2)]
        wkv = [sbt(f"wkv{i}", [128, E], BF16) for i in range(2)]
        x_g = [sbt(f"x_g{i}", [128, 4 * TW], BF16) for i in range(2)]
        xn_sub = sbt("xn_sub", [128, NCHUNK * TW], BF16)
        attn_t = sbt("attn_t", [128, H * TW], BF16)
        t_f32 = sbt("t_f32", [128, TW], F32)
        kc_f32 = sbt("kc_f32", [128, TW], F32)
        km_f32 = sbt("km_f32", [128, TW], F32)
        q_sb = sbt("q_sb", [128, TW], BF16)
        q_rope = sbt("q_rope", [128, TW], BF16)
        p_sb = sbt("p_sb", [128, 4 * TW], BF16)
        linv_b = sbt("linv_b", [128, TW], F32)
        x_res = sbt("x_res", [128, TW], F32)
        out_sb = sbt("out_sb", [128, TW], F32)
        eps_c = sbt("eps_c", [128, 1], F32)
        ktab_s, freq_s, red_s = t_f32, kc_f32, km_f32
        sq_s = q_sb
        k_tmp = q_rope
        linv_r = t_f32[0:1, :]
        h_sb = p_sb[:, 0:TW]
        mu_r = x_res[0:1, :]
        b_r = out_sb[0:1, :]
        c_r = linv_b[0:1, :]
        invf = linv_b[:, 0:1]

        pq = es.enter_context(nc.psum_tensor("pq", [128, 512], F32))
        prot = es.enter_context(nc.psum_tensor("prot", [128, 512], F32))
        psc0 = es.enter_context(nc.psum_tensor("psc0", [128, 1024], F32))
        psc1 = es.enter_context(nc.psum_tensor("psc1", [128, 1024], F32))
        pl = es.enter_context(nc.psum_tensor("pl", [128, 512], F32))
        pattn = es.enter_context(nc.psum_tensor("pattn", [128, 512], F32))
        psc = [psc0, psc1]

        def dma(dst, src):
            return lambda eng, d=dst, s=src: eng.dma_start(d, s)

        t_c = 0
        for dst, src in ((rt_sb[:, :], rt[:, :]), (masks_sb[:, :], masks[:, :]),
                         (bias_sb[:, :], bias_t[:, :]), (coef_sb[:, :], coef[:, :])):
            t_c = p.emit("y", dma(dst, src), inc="sd_const", inc_amt=16)

        t_ones_l = p.emit("v", lambda e: e.memset(ones_l[:, :], 1.0), inc="ve")
        t_eps = p.emit("v", lambda e: e.memset(eps_c[:, :], EPS), inc="ve")
        t_ones_b = p.emit("v", lambda e: e.memset(ones_b[:, :], 1.0), inc="ve")

        t_iota_d = p.emit("g", lambda e: e.iota(
            invf[:, :], pattern=[[0, 1]], base=0, channel_multiplier=1,
            allow_small_or_imprecise_dtypes=True), inc="po")
        c_exp = float(-np.log(10000.0) / 64.0)
        t_invf = p.emit("s", lambda e: e.activation(invf[:, :], invf[:, :], AF.Exp, scale=c_exp),
                        waits=[("po", t_iota_d)], inc="sc")
        t_invf2 = p.emit("v", lambda e: e.tensor_scalar(
            invf[64:128, :], invf[64:128, :], 10000.0, None, OP.mult),
            waits=[("sc", t_invf)], inc="ve")

        def cascade(extra_waits, tok0):
            tok = tok0
            for i, cc in enumerate((CW1, CW2, CW3)):
                use_src = (i == 0)
                def f(e, cc=cc, use_src=use_src):
                    return e.scalar_tensor_tensor(
                        red_s[:, :], ktab_s[:, :], float(-cc),
                        freq_s[:, :] if use_src else red_s[:, :], OP.mult, OP.add)
                w = [("ve", tok)] + (list(extra_waits) if i == 0 else [])
                tok = p.emit("v", f, waits=w, inc="ve")
            return p.emit("v", lambda e: e.tensor_scalar(
                red_s[:, :], red_s[:, :], PI_F, -PI_F, OP.min, OP.max),
                waits=[("ve", tok)], inc="ve")

        t_sin_done = 0
        t_cos_done = 0
        t_prev_sin_sc = 0
        t_prev_ve = t_invf2
        t_prev_k = 0
        for st in range(4):
            cols = slice(st * TW, (st + 1) * TW)
            t_i = p.emit("g", lambda e, st=st: e.iota(
                freq_s[:, :], pattern=[[1, TW]], base=st * TW, channel_multiplier=0,
                allow_small_or_imprecise_dtypes=True),
                waits=[("ve", t_prev_ve)], inc="po")
            t_fr = p.emit("v", lambda e: e.tensor_scalar(
                freq_s[:, :], freq_s[:, :], invf[:, 0:1], None, OP.mult),
                waits=[("po", t_i), ("ve", t_invf2)], inc="ve")
            t_k1 = p.emit("y", dma(ktab_s[:, :], k1[:, cols]),
                          waits=[("ve", t_prev_k)], inc="sd_ktab", inc_amt=16)
            t_r1 = cascade([("sd_ktab", t_k1), ("sc", t_prev_sin_sc)], t_fr)
            t_sin_done = p.emit("s", lambda e, cols=cols: e.activation(
                sin_t[:, cols], red_s[:, :], AF.Sin),
                waits=[("ve", t_r1), ("sc", t_prev_sin_sc)], inc="sc")
            t_f2 = p.emit("v", lambda e: e.tensor_scalar(
                freq_s[:, :], freq_s[:, :], float(np.pi / 2), None, OP.add),
                waits=[("ve", t_r1)], inc="ve")
            t_k2 = p.emit("y", dma(ktab_s[:, :], k2[:, cols]),
                          waits=[("ve", t_r1)], inc="sd_ktab", inc_amt=16)
            t_r2 = cascade([("sd_ktab", t_k2), ("sc", t_sin_done)], t_f2)
            t_cos_done = p.emit("s", lambda e, cols=cols: e.activation(
                cos_t[:, cols], red_s[:, :], AF.Sin),
                waits=[("ve", t_r2)], inc="sc")
            t_prev_sin_sc = t_cos_done
            t_prev_ve = t_r2
            t_prev_k = t_r2
        t_tables = t_cos_done

        tok_stats = {}
        tok_sq = {}
        tok_xg = {}
        t_xn_grp = {}
        tok_xnw = {}
        tok_vcons = {}
        t_vevac_all = 0
        t_vevac_all_prev = 0
        kr = {}
        kv_gen = 0
        tok_wkv = {}
        tok_wkv_used = {}
        t1_of = {}
        t2_of = {}
        last_sq_g = [0]
        last_ssq_g = [0]

        def xg_load(slot, grp, sub):
            rows = slice(grp * 512, (grp + 1) * 512)
            cols = slice(sub * TW, (sub + 1) * TW)
            src = xT[rows, cols].rearrange("(c p) s -> p c s", p=128)
            dst = x_g[slot][:, :].rearrange("p (c s) -> p c s", s=TW)
            return lambda e, d=dst, s=src: e.dma_start(d, s)

        NG = 8

        def emit_xg_load(sub, pas, grp, gates):
            slot = grp % 2
            tok_xg[(sub, pas, grp)] = p.emit(
                "g", xg_load(slot, grp, sub), waits=gates,
                inc=f"sd_xg{slot}", inc_amt=16)

        def emit_stats_grp(sub, grp):
            slot = grp % 2
            wl = [(f"sd_xg{slot}", tok_xg[(sub, "A", grp)])]
            last = 0
            for c in range(4):
                cc = grp * 4 + c
                def f_sum(e, slot=slot, c=c, cc=cc):
                    return e.matmul(pl[0:1, :], ones_l[:, :],
                                    x_g[slot][:, c * TW:(c + 1) * TW],
                                    start=(cc == 0), stop=(cc == 31))
                w = list(wl) if c == 0 else []
                if cc == 0:
                    w += [("ve", t_ones_l)]
                    if sub > 0:
                        w += [("ve", t1_of[sub - 1])]
                p.emit("t", f_sum, waits=w)
                wsq = list(wl) if c == 0 else []
                if last_ssq_g[0]:
                    wsq.append(("pe", last_ssq_g[0]))
                wsq.append(("ve", last_sq_g[0]))
                last_sq_g[0] = p.emit("v", lambda e, slot=slot, c=c: e.tensor_tensor(
                    sq_s[:, :], x_g[slot][:, c * TW:(c + 1) * TW],
                    x_g[slot][:, c * TW:(c + 1) * TW], OP.mult),
                    waits=[x for x in wsq if x[1]], inc="ve")
                def f_ssq(e, cc=cc):
                    return e.matmul(prot[0:1, :], ones_l[:, :],
                                    sq_s[:, :], start=(cc == 0), stop=(cc == 31))
                w2 = [("ve", last_sq_g[0])]
                if cc == 0 and sub > 0:
                    w2.append(("ve", t_xn_grp[(sub - 1, NG - 1)]))
                last_ssq_g[0] = p.emit("t", f_ssq, waits=w2, inc="pe")
                last = last_ssq_g[0]
            tok_stats[(sub, grp)] = last
            tok_sq[(sub, grp)] = last_sq_g[0]
            if grp + 2 < NG:
                emit_xg_load(sub, "A", grp + 2,
                             [("pe", tok_stats[(sub, grp)]),
                              ("ve", tok_sq[(sub, grp)])])

        early_wq = {}
        for h0 in (0, 1):
            early_wq[h0] = p.emit("g", dma(wbig[h0][:, :], wq[h0, :, :]),
                                  inc=f"sd_wbg{h0}", inc_amt=16)
        wbk_kv = {}

        for grp in (0, 1):
            emit_xg_load(0, "A", grp, [])
        for grp in range(NG):
            emit_stats_grp(0, grp)

        bc_toks = {}
        xn_chunk_tok = {}

        def emit_rows(sub):
            t_stats = tok_stats[(sub, NG - 1)]
            t1 = p.emit("v", lambda e: e.tensor_scalar(
                mu_r[:, :], pl[0:1, :], float(1.0 / E), None, OP.mult),
                waits=[("pe", t_stats), ("sc", t_tables)], inc="ve")
            t1_of[sub] = t1
            t2 = p.emit("v", lambda e: e.tensor_scalar(
                b_r[:, :], prot[0:1, :], float(1.0 / E), None, OP.mult),
                waits=[("ve", t1)], inc="ve")
            t2_of[sub] = t2
            t3 = p.emit("v", lambda e: e.tensor_tensor(
                c_r[:, :], mu_r[:, :], mu_r[:, :], OP.mult),
                waits=[("ve", t2)], inc="ve")
            t4 = p.emit("v", lambda e: e.tensor_tensor(
                b_r[:, :], b_r[:, :], c_r[:, :], OP.subtract),
                waits=[("ve", t3)], inc="ve")
            t5 = p.emit("s", lambda e: e.activation(
                b_r[:, :], b_r[:, :], AF.Ln, bias=eps_c[0:1, 0:1]),
                waits=[("ve", t4), ("ve", t_eps), ("sc", t_tables)], inc="sc")
            t6 = p.emit("s", lambda e: e.activation(
                b_r[:, :], b_r[:, :], AF.Exp, scale=-0.5),
                waits=[("sc", t5)], inc="sc")
            t7 = p.emit("v", lambda e: e.tensor_tensor(
                c_r[:, :], mu_r[:, :], b_r[:, :], OP.mult),
                waits=[("sc", t6)], inc="ve")
            bc_toks[(sub, "t7")] = t7
            for grp in (0, 1):
                emit_xg_load(sub, "B", grp,
                             [("pe", tok_stats[(sub, NG - 2 + grp)]),
                              ("ve", tok_sq[(sub, NG - 2 + grp)])])

        def emit_bc(sub):
            t7 = bc_toks[(sub, "t7")]
            wbc = [("ve", t7), ("ve", t_ones_b)]
            if sub > 0:
                wbc.append(("ve", t_xn_grp[(sub - 1, NG - 1)]))
                wbc.append(("ve", kr[("m", sub - 1, KV - 1)]))
            bc_toks[(sub, "r")] = p.emit("t", lambda e: e.matmul(
                pq[:, :], ones_b[:, :], b_r[:, :], start=True, stop=True),
                waits=wbc, inc="pe")
            bc_toks[(sub, "mr")] = p.emit("t", lambda e: e.matmul(
                prot[:, :], ones_b[:, :], c_r[:, :], start=True, stop=True),
                waits=[("ve", t7), ("ve", t2_of[sub])], inc="pe")

        def emit_xn_grp(sub, grp, pe_gate):
            slot = grp % 2
            wl = [(f"sd_xg{slot}", tok_xg[(sub, "B", grp)]),
                  ("pe", bc_toks[(sub, "r")]), ("pe", bc_toks[(sub, "mr")]),
                  ("sc", t_tables)]
            if pe_gate:
                wl.append(("pe", pe_gate))
            prev_xn = xn_chunk_tok.get((sub, grp * 4 - 1), 0)
            for c in range(4):
                cc = grp * 4 + c
                w = list(wl) if c == 0 else []
                if prev_xn:
                    w.append(("ve", prev_xn))
                tt = p.emit("v", lambda e, slot=slot, c=c: e.tensor_tensor(
                    t_f32[:, :], x_g[slot][:, c * TW:(c + 1) * TW],
                    pq[:, :], OP.mult), waits=[x for x in w if x[1]], inc="ve")
                wx = [("ve", tt)]
                if cc == 0 and sub > 0:
                    wx.append(("sd_xnw", tok_xnw[sub - 1]))
                prev_xn = p.emit("v", lambda e, cc=cc: e.tensor_tensor(
                    xn_sub[:, cc * TW:(cc + 1) * TW], t_f32[:, :],
                    prot[:, :], OP.subtract), waits=wx, inc="ve")
                xn_chunk_tok[(sub, cc)] = prev_xn
            t_xn_grp[(sub, grp)] = prev_xn
            if grp + 2 < NG:
                emit_xg_load(sub, "B", grp + 2,
                             [("ve", t_xn_grp[(sub, grp)])])

        def emit_xnw(sub):
            tok_xnw[sub] = p.emit(
                "y", lambda e, sub=sub: e.dma_start(xn_scr[sub, :, :], xn_sub[:, :]),
                waits=[("ve", t_xn_grp[(sub, NG - 1)])],
                inc="sd_xnw", inc_amt=16)

        emit_rows(0)
        emit_bc(0)
        for grp in range(NG):
            emit_xn_grp(0, grp, 0)
        emit_xnw(0)

        for sub in range(4):
            cols = slice(sub * TW, (sub + 1) * TW)
            t_xn_done = t_xn_grp[(sub, NG - 1)]

            for oc in range(KV):
                slot = kv_gen % 2
                gates = []
                if kv_gen >= 2:
                    gates.append(("pe", tok_wkv_used[kv_gen - 2]))
                if kv_gen >= 2 and (kv_gen - 2) in wbk_kv:
                    gates.append(wbk_kv[kv_gen - 2])
                if sub == 0:
                    tok_half = p.emit(
                        "g", dma(wkv[slot][:, 0:E // 2], wk[oc, :, 0:E // 2]),
                        waits=gates, inc=f"sd_kv{slot}", inc_amt=16)
                    tok_wkv[(kv_gen, "h1")] = (f"sd_kv2{slot}", p.emit(
                        "g", dma(wkv[slot][:, E // 2:], wk[oc, :, E // 2:]),
                        inc=f"sd_kv2{slot}", inc_amt=16))
                    tok_wkv[kv_gen] = (f"sd_kv{slot}", tok_half)
                    wbk_kv[kv_gen] = (f"sd_wkbk{slot}", p.emit(
                        "y", dma(wk_bf[oc, :, :], wkv[slot][:, :]),
                        waits=[tok_wkv[kv_gen], tok_wkv[(kv_gen, "h1")]],
                        inc=f"sd_wkbk{slot}", inc_amt=16))
                    wbk_kv[("k", oc)] = wbk_kv[kv_gen]
                else:
                    gates.append(wbk_kv[("k", oc)])
                    tok_wkv[kv_gen] = (f"sd_kvh{slot}", p.emit(
                        "y", dma(wkv[slot][:, :], wk_bf[oc, :, :]),
                        waits=gates, inc=f"sd_kvh{slot}", inc_amt=16))
                this_gen = kv_gen
                kv_gen += 1
                kp = pattn[:, :] if oc % 2 == 0 else psc1[:, 0:512]
                last = 0
                for c in range(NCHUNK):
                    w = []
                    if oc == 0:
                        w.append(("ve", xn_chunk_tok[(sub, c)]))
                    if c == 0:
                        w += [tok_wkv[this_gen]]
                    elif c == 16 and (this_gen, "h1") in tok_wkv:
                        w.append(tok_wkv[(this_gen, "h1")])
                        if oc >= 2:
                            w.append(("sc", kr[("sc", sub, oc - 2)]))
                        elif sub > 0:
                            w.append(("sc", kr[("sc", sub - 1, KV - 2 + oc)]))
                            w.append(("sc", t_vevac_all_prev))
                        if oc == 1:
                            w.append(("ve", t_xn_done))
                    last = p.emit("t", lambda e, slot=slot, c=c, kp=kp: e.matmul(
                        kp, wkv[slot][:, c * 128:(c + 1) * 128],
                        xn_sub[:, c * TW:(c + 1) * TW],
                        start=(c == 0), stop=(c == NCHUNK - 1)),
                        waits=w, inc="pe" if c == NCHUNK - 1 else None)
                tok_wkv_used[this_gen] = last
                prev_rot = kr.get(("rot", sub, oc - 1), kr.get(("rot", sub - 1, KV - 1), 0))
                prev_c1 = kr.get(("c1", sub, oc - 1), kr.get(("c1", sub - 1, KV - 1), 0))
                prev_m = kr.get(("m", sub, oc - 1), kr.get(("m", sub - 1, KV - 1), 0))
                prev_add = kr.get(("add", sub, oc - 1), kr.get(("add", sub - 1, KV - 1), 0))
                kr[("sc", sub, oc)] = p.emit(
                    "s", lambda e, kp=kp: e.activation(k_tmp[:, :], kp, AF.Copy),
                    waits=[x for x in [("pe", last), ("pe", prev_rot), ("ve", prev_c1)] if x[1]],
                    inc="sc")
                w_rot = [("sc", kr[("sc", sub, oc)]), ("sd_const", t_c), ("ve", prev_m)]
                kr[("rot", sub, oc)] = p.emit(
                    "t", lambda e: e.matmul(pq[:, :], rt_sb[:, :], k_tmp[:, :],
                                            start=True, stop=True),
                    waits=[x for x in w_rot if x[1]], inc="pe")
                kr[("c1", sub, oc)] = p.emit(
                    "v", lambda e, cols=cols: e.tensor_tensor(
                        kc_f32[:, :], k_tmp[:, :], cos_t[:, cols], OP.mult),
                    waits=[x for x in [("sc", kr[("sc", sub, oc)]), ("sc", t_tables),
                                       ("ve", prev_add)] if x[1]], inc="ve")
                kr[("m", sub, oc)] = p.emit(
                    "v", lambda e, cols=cols: e.tensor_tensor(
                        km_f32[:, :], pq[:, :], sin_t[:, cols], OP.mult),
                    waits=[("pe", kr[("rot", sub, oc)]), ("ve", kr[("c1", sub, oc)])],
                    inc="ve")
                kr[("add", sub, oc)] = p.emit(
                    "v", lambda e, oc=oc, sub=sub: e.tensor_tensor(
                        K_sb[:, oc * S + sub * TW: oc * S + (sub + 1) * TW],
                        kc_f32[:, :], km_f32[:, :], OP.add),
                    waits=[("ve", kr[("m", sub, oc)])], inc="ve")
                if sub + 1 < 4:
                    if oc == 0:
                        for grp in (0, 1):
                            emit_xg_load(sub + 1, "A", grp,
                                         [("ve", t_xn_grp[(sub, NG - 2 + grp)])])
                    if oc % 2 == 1:
                        emit_stats_grp(sub + 1, oc // 2)

            vps = [psc0[:, 0:512], psc0[:, 512:1024], psc1[:, 0:512], psc1[:, 512:1024]]
            t_vev = {}
            for half in range(2):
                wv_tok = {}
                wv_slot = {}
                wv_genid = {}

                def emit_wv_load(wg, half):
                    nonlocal kv_gen
                    slot = kv_gen % 2
                    gates = []
                    if kv_gen >= 2:
                        gates.append(("pe", tok_wkv_used[kv_gen - 2]))
                    if kv_gen >= 2 and (kv_gen - 2) in wbk_kv:
                        gates.append(wbk_kv[kv_gen - 2])
                    if sub == 0:
                        def f_wv(e, wg=wg, half=half, slot=slot, part=0):
                            src = wvt[wg * 8 + part * 4:wg * 8 + part * 4 + 4, :,
                                      half * 512:(half + 1) * 512]
                            dst = wkv[slot][:, part * 2048:(part + 1) * 2048].rearrange(
                                "p (c s) -> p c s", s=512)
                            return e.dma_start(dst, src.rearrange("c p s -> p c s"))
                        t_h0 = p.emit("g", lambda e, f=f_wv: f(e), waits=gates,
                                      inc=f"sd_kv{slot}", inc_amt=16)
                        def f_wv1(e, wg=wg, half=half, slot=slot):
                            src = wvt[wg * 8 + 4:wg * 8 + 8, :,
                                      half * 512:(half + 1) * 512]
                            dst = wkv[slot][:, 2048:4096].rearrange(
                                "p (c s) -> p c s", s=512)
                            return e.dma_start(dst, src.rearrange("c p s -> p c s"))
                        wv_tok[(wg, "h1")] = (f"sd_kv2{slot}", p.emit(
                            "g", f_wv1, inc=f"sd_kv2{slot}", inc_amt=16))
                        wv_tok[wg] = (f"sd_kv{slot}", t_h0)
                        wbk_kv[kv_gen] = (f"sd_wkbk{slot}", p.emit(
                            "y", lambda e, wg=wg, half=half, slot=slot: e.dma_start(
                                wv_bf[wg, half, :, :], wkv[slot][:, :]),
                            waits=[wv_tok[wg], wv_tok.get((wg, "h1"), (None, 0))],
                            inc=f"sd_wkbk{slot}", inc_amt=16))
                        wbk_kv[("v", wg, half)] = wbk_kv[kv_gen]
                    else:
                        gates.append(wbk_kv[("v", wg, half)])
                        wv_tok[wg] = (f"sd_kvh{slot}", p.emit(
                            "y", lambda e, wg=wg, half=half, slot=slot: e.dma_start(
                                wkv[slot][:, :], wv_bf[wg, half, :, :]),
                            waits=gates, inc=f"sd_kvh{slot}", inc_amt=16))
                    wv_slot[wg] = slot
                    wv_genid[wg] = kv_gen
                    kv_gen += 1

                for wg in (0, 1):
                    emit_wv_load(wg, half)
                last_mm = 0
                for wg in range(4):
                    slot = wv_slot[wg]
                    for c8 in range(8):
                        c = wg * 8 + c8
                        for ts in range(4):
                            w = []
                            if c8 == 0 and ts == 0:
                                w = [wv_tok[wg]]
                                if c == 0:
                                    if half == 0:
                                        w += [("ve", t2_of[sub]), ("sc", kr[("sc", sub, KV - 1)]),
                                              ("ve", t_xn_done)]
                                    else:
                                        w += [("sc", t_vev[(0, 3)])]
                            elif c8 == 4 and ts == 0 and (wg, "h1") in wv_tok:
                                w = [wv_tok[(wg, "h1")]]
                            inc = "pe" if (c8 == 7 and ts == 3) else None
                            r = p.emit("t", lambda e, slot=slot, c8=c8, c=c, ts=ts, vp=vps[ts]: e.matmul(
                                vp, xn_sub[:, c * TW + ts * 128: c * TW + (ts + 1) * 128],
                                wkv[slot][:, c8 * 512:(c8 + 1) * 512],
                                start=(c == 0), stop=(c == NCHUNK - 1)),
                                waits=w, inc=inc)
                            if r:
                                last_mm = r
                    tok_wkv_used[wv_genid[wg]] = last_mm
                    if wg + 2 < 4:
                        emit_wv_load(wg + 2, half)
                    if sub + 1 < 4:
                        if half == 0:
                            emit_stats_grp(sub + 1, 4 + wg)
                            if wg == 3:
                                emit_rows(sub + 1)
                        else:
                            if wg == 0:
                                emit_bc(sub + 1)
                            emit_xn_grp(sub + 1, 2 * wg, last_mm)
                            emit_xn_grp(sub + 1, 2 * wg + 1, 0)
                for ts in range(4):
                    kt_glob = sub * 4 + ts
                    t_vev[(half, ts)] = p.emit(
                        "s", lambda e, vp=vps[ts], half=half, kt_glob=kt_glob: e.activation(
                            V_sb[:, kt_glob * 1024 + half * 512: kt_glob * 1024 + (half + 1) * 512],
                            vp, AF.Copy),
                        waits=[("pe", last_mm)], inc="sc")
            t_vevac_all = t_vev[(1, 3)]
            t_vevac_all_prev = t_vevac_all
            tok_vcons[sub] = last_mm
            if sub + 1 < 4:
                emit_xnw(sub + 1)

        w_lora_gate = [("pe", tok_wkv_used[kv_gen - 1]), ("pe", tok_wkv_used[kv_gen - 2])]
        p.emit("g", dma(wkv[0][:, :], wa[:, :]), waits=w_lora_gate,
               inc="sd_lora", inc_amt=16)
        t_lora = p.emit("g", dma(wkv[1][:, :], wb[:, :]), inc="sd_lora", inc_amt=16)
        wa_s, wb_s = wkv[0], wkv[1]

        gh_q = {}
        tok_qevac = {}
        tok_rot = {}
        tok_c1 = {}
        tok_m = {}
        tok_add = {}
        tok_scores = {}
        tok_pv = {}
        tok_l = {}
        tok_recip = {}
        tok_bc = {}
        tok_bcevac = {}
        tok_attnmul = {}
        pair_exp = {}
        tok_out = {}
        tok_stt = {}
        tok_store = {}
        tok_xres = {}
        tok_wbig = {}
        wbk_q = {}
        wbk_ow = {}
        tok_xn_load = {}
        state = {"pair": 0}
        prev_use_slot = {0: 0, 1: 0}
        prev_wbk_slot = {0: 0, 1: 0}
        prev_out_last = 0
        t_hs = 0
        t_hs_prev = 0

        def wq_load(qst, h):
            slot = h % 2
            if qst == 0 and h in (0, 1):
                tok_wbig[(qst, "q", h)] = (f"sd_wbg{slot}", early_wq[h])
                return
            gates = []
            if prev_use_slot[slot]:
                gates.append(("pe", prev_use_slot[slot]))
            if prev_wbk_slot[slot]:
                gates.append((f"sd_wbk{slot}", prev_wbk_slot[slot]))
            if qst == 0:
                sem = f"sd_wbg{slot}"
                tok = p.emit("g", dma(wbig[slot][:, :], wq[h, :, :]),
                             waits=gates, inc=sem, inc_amt=16)
            else:
                sem = f"sd_wb{slot}"
                gates.append((f"sd_wbk{h % 2}", wbk_q[h]))
                tok = p.emit("y", dma(wbig[slot][:, :], wq_bf[h, :, :]),
                             waits=gates, inc=sem, inc_amt=16)
            tok_wbig[(qst, "q", h)] = (sem, tok)

        def wow_load(qst, oc, half):
            u = 2 * oc + half
            slot = u % 2
            gates = []
            if prev_use_slot[slot]:
                gates.append(("pe", prev_use_slot[slot]))
            if prev_wbk_slot[slot]:
                gates.append((f"sd_wbk{slot}", prev_wbk_slot[slot]))
            if qst == 0:
                sem = f"sd_wbg{slot}"
                src = wo[oc, :, :] if half == 0 else wf[oc, :, :]
                tok = p.emit("g", dma(wbig[slot][:, :], src),
                             waits=gates, inc=sem, inc_amt=16)
            else:
                sem = f"sd_wb{slot}"
                gates.append((f"sd_wbk{slot}", wbk_ow[u]))
                tok = p.emit("y", dma(wbig[slot][:, :],
                                      wow_bf[oc, :, half * E:(half + 1) * E]),
                             waits=gates, inc=sem, inc_amt=16)
            tok_wbig[(qst, "ow", u)] = (sem, tok)

        def xres_load(qst, oc):
            gates = []
            if (qst, oc - 1) in tok_stt:
                gates.append(("ve", tok_stt[(qst, oc - 1)]))
            elif qst > 0:
                gates.append(("ve", tok_stt[(qst - 1, NCHUNK - 1)]))
            tok_xres[(qst, oc)] = p.emit(
                "y", lambda e, qst=qst, oc=oc: e.dma_start(
                    x_res[:, :],
                    xT[oc * 128:(oc + 1) * 128, qst * TW:(qst + 1) * TW]),
                waits=gates, inc="sd_xres", inc_amt=16)

        def head_tail(qst, h, fill=None):
            gh = qst * H + h
            if fill is None:
                fill = lambda n: None
            kts = KT_SCHED[qst]
            nk = len(kts)
            npair = nk // 2
            pb = state["pair"]
            kv = h // 4
            t_exp = {}
            t_mask = {}
            pv_done = {}

            def emit_lpv(j):
                slot = (pb + j) % 2
                last_l = last_pv = 0
                for half in range(2):
                    pos = 2 * j + half
                    kt = kts[pos][0]
                    sl = slice(slot * 1024 + half * 512, slot * 1024 + (half + 1) * 512)
                    w = []
                    if half == 0:
                        w = [("sc", t_exp[j]), ("ve", t_ones_l)]
                        if t_mask.get(j):
                            w.append(("ve", t_mask[j]))
                        if j == 0:
                            if gh > 0:
                                w.append(("ve", tok_bcevac[gh - 1]))
                            else:
                                w.append(("ve", kr[("add", 3, KV - 1)]))
                            if h == 0 and qst > 0:
                                w.append(("ve", t_hs_prev))
                    last_l = p.emit("t", lambda e, sl=sl, pos=pos: e.matmul(
                        pl[0:1, :], ones_l[:, :], p_sb[:, sl],
                        start=(pos == 0), stop=(pos == nk - 1)),
                        waits=w, inc="pe" if pos == nk - 1 else None)
                    w2 = []
                    if half == 0 and j == 0:
                        w2 = [("sc", t_exp[j])]
                        if t_mask.get(j):
                            w2.append(("ve", t_mask[j]))
                        if gh > 0:
                            w2.append(("ve", tok_attnmul[gh - 1]))
                        else:
                            w2.append(("sc", kr[("sc", 3, KV - 1)]))
                    last_pv = p.emit("t", lambda e, sl=sl, pos=pos, kt=kt, kv=kv: e.matmul(
                        pattn[:, :], V_sb[:, kt * 1024 + kv * 128: kt * 1024 + (kv + 1) * 128],
                        p_sb[:, sl], start=(pos == 0), stop=(pos == nk - 1)),
                        waits=w2, inc="pe" if half == 1 else None)
                pv_done[j] = last_pv
                return last_l, last_pv

            last_l = last_pv = 0
            for j in range(npair):
                fill(4)
                pidx = pb + j
                par = pidx % 2
                slot = pidx % 2
                t_mm = 0
                for half in range(2):
                    kt = kts[2 * j + half][0]
                    w = []
                    if half == 0:
                        w = [("ve", tok_add[gh])]
                        if pidx >= 2:
                            w.append(("sc", pair_exp[pidx - 2]))
                        else:
                            w.append(("sc", t_vevac_all))
                    t_mm = p.emit("t", lambda e, par=par, half=half, kt=kt, kv=kv: e.matmul(
                        psc[par][:, half * 512:(half + 1) * 512],
                        K_sb[:, kv * S + kt * 128: kv * S + (kt + 1) * 128],
                        q_rope[:, :], start=True, stop=True),
                        waits=w, inc="pe" if half == 1 else None)
                tok_scores[gh] = t_mm
                w_e = [("pe", t_mm)]
                if j >= 2:
                    w_e.append(("pe", pv_done[j - 2]))
                elif gh > 0:
                    w_e.append(("pe", tok_pv[gh - 1]))
                t_exp[j] = p.emit("s", lambda e, par=par, slot=slot: e.activation(
                    p_sb[:, slot * 1024:(slot + 1) * 1024], psc[par][:, :], AF.Exp),
                    waits=w_e, inc="sc")
                pair_exp[pidx] = t_exp[j]
                tm = 0
                for half in range(2):
                    pos = 2 * j + half
                    mi = kts[pos][1]
                    if mi is None:
                        continue
                    sl = slice(slot * 1024 + half * 512, slot * 1024 + (half + 1) * 512)
                    w = [("sc", t_exp[j]), ("sd_const", t_c)]
                    if tm:
                        w.append(("ve", tm))
                    tm = p.emit("v", lambda e, sl=sl, mi=mi: e.tensor_tensor(
                        p_sb[:, sl], p_sb[:, sl],
                        masks_sb[:, mi * 512:(mi + 1) * 512], OP.mult),
                        waits=w, inc="ve")
                t_mask[j] = tm
                if j >= 1:
                    fill(4)
                    last_l, last_pv = emit_lpv(j - 1)
            fill(4)
            last_l, last_pv = emit_lpv(npair - 1)
            state["pair"] = pb + npair
            tok_l[gh] = last_l
            tok_pv[gh] = last_pv
            fill(8)
            tok_recip[gh] = p.emit("v", lambda e: e.reciprocal(linv_r[:, :], pl[0:1, :]),
                                   waits=[("pe", last_l)], inc="ve")
            tok_bc[gh] = p.emit("t", lambda e: e.matmul(
                pl[:, :], ones_b[:, :], linv_r[:, :], start=True, stop=True),
                waits=[("ve", tok_recip[gh]), ("ve", t_ones_b)], inc="pe")
            tok_bcevac[gh] = p.emit("v", lambda e: e.tensor_copy(linv_b[:, :], pl[:, :]),
                                    waits=[("pe", tok_bc[gh])], inc="ve")
            tok_attnmul[gh] = p.emit("v", lambda e, h=h: e.tensor_tensor(
                attn_t[:, h * TW:(h + 1) * TW], pattn[:, :], linv_b[:, :], OP.mult),
                waits=[("pe", last_pv), ("ve", tok_bcevac[gh])], inc="ve")

        for qst in range(NQST):
            cols = slice(qst * TW, (qst + 1) * TW)
            w_xn = [("sd_xnw", tok_xnw[3])]
            if qst == 0:
                w_xn.append(("pe", tok_vcons[3]))
            else:
                w_xn.append(("pe", prev_out_last))
            tok_xn_half = p.emit(
                "y", lambda e, qst=qst: e.dma_start(
                    xn_sub[:, 0:16 * TW], xn_scr[qst, :, 0:16 * TW]),
                waits=w_xn, inc="sd_xnr", inc_amt=16)
            tok_xn_load[qst] = p.emit(
                "y", lambda e, qst=qst: e.dma_start(
                    xn_sub[:, 16 * TW:], xn_scr[qst, :, 16 * TW:]),
                inc="sd_xnr2", inc_amt=16)

            for h in (0, 1):
                wq_load(qst, h)

            for h in range(H):
                gh = qst * H + h
                slot = h % 2
                qstate = {"c": 0, "last": 0}

                def fill_q(n, h=h, gh=gh, slot=slot, qstate=qstate, qst=qst,
                           txh=tok_xn_half):
                    while n > 0 and qstate["c"] < NCHUNK:
                        c = qstate["c"]
                        w = []
                        if c == 0:
                            w = [tok_wbig[(qst, "q", h)], ("sd_xnr", txh)]
                            if gh > 0:
                                w.append(("sc", tok_qevac[gh - 1]))
                            if h == 0:
                                if qst == 0:
                                    w.append(("ve", kr[("m", 3, KV - 1)]))
                                else:
                                    w.append(("ve", tok_stt[(qst - 1, NCHUNK - 1)]))
                        elif c == 16:
                            w = [("sd_xnr2", tok_xn_load[qst])]
                        qstate["last"] = p.emit("t", lambda e, slot=slot, c=c: e.matmul(
                            pq[:, :], wbig[slot][:, c * 128:(c + 1) * 128],
                            xn_sub[:, c * TW:(c + 1) * TW],
                            start=(c == 0), stop=(c == NCHUNK - 1)),
                            waits=w, inc="pe" if c == NCHUNK - 1 else None)
                        qstate["c"] += 1
                        n -= 1

                if h > 0:
                    head_tail(qst, h - 1, fill_q)
                fill_q(NCHUNK)
                gh_q[gh] = qstate["last"]
                prev_use_slot[slot] = qstate["last"]

                if qst == 0:
                    wbk_q[h] = p.emit(
                        "y", dma(wq_bf[h, :, :], wbig[slot][:, :]),
                        waits=[tok_wbig[(qst, "q", h)]],
                        inc=f"sd_wbk{slot}", inc_amt=16)
                    prev_wbk_slot[slot] = wbk_q[h]
                if h + 2 < H:
                    wq_load(qst, h + 2)

                w_ev = [("pe", gh_q[gh])]
                if gh > 0:
                    w_ev += [("pe", tok_rot[gh - 1]), ("ve", tok_c1[gh - 1])]
                tok_qevac[gh] = p.emit("s", lambda e: e.activation(
                    q_sb[:, :], pq[:, :], AF.Copy, scale=INV_SQRT_D),
                    waits=w_ev, inc="sc")

                w_rot = [("sc", tok_qevac[gh])]
                if gh > 0:
                    w_rot.append(("ve", tok_m[gh - 1]))
                else:
                    w_rot.append(("ve", kr[("m", 3, KV - 1)]))
                if h == 0 and qst > 0:
                    w_rot.append(("ve", tok_stt[(qst - 1, NCHUNK - 1)]))
                tok_rot[gh] = p.emit("t", lambda e: e.matmul(
                    prot[:, :], rt_sb[:, :], q_sb[:, :], start=True, stop=True),
                    waits=w_rot, inc="pe")
                w_c1 = [("sc", tok_qevac[gh])]
                if gh > 0:
                    w_c1.append(("ve", tok_add[gh - 1]))
                else:
                    w_c1.append(("ve", kr[("add", 3, KV - 1)]))
                tok_c1[gh] = p.emit("v", lambda e, cols=cols: e.tensor_tensor(
                    kc_f32[:, :], q_sb[:, :], cos_t[:, cols], OP.mult),
                    waits=w_c1, inc="ve")
                tok_m[gh] = p.emit("v", lambda e, cols=cols: e.tensor_tensor(
                    km_f32[:, :], prot[:, :], sin_t[:, cols], OP.mult),
                    waits=[("pe", tok_rot[gh]), ("ve", tok_c1[gh])], inc="ve")
                w_add = [("ve", tok_m[gh])]
                if gh > 0:
                    w_add.append(("pe", tok_scores[gh - 1]))
                tok_add[gh] = p.emit("v", lambda e: e.tensor_tensor(
                    q_rope[:, :], kc_f32[:, :], km_f32[:, :], OP.add),
                    waits=w_add, inc="ve")

            head_tail(qst, H - 1)
            gh_last = qst * H + H - 1

            w_h = [("ve", tok_bcevac[gh_last]), ("sd_lora", t_lora),
                   ("sd_xnr", tok_xn_half), ("sd_xnr2", tok_xn_load[qst])]
            last = 0
            for c in range(NCHUNK):
                last = p.emit("t", lambda e, c=c: e.matmul(
                    pl[:, :], wa_s[:, c * 128:(c + 1) * 128],
                    xn_sub[:, c * TW:(c + 1) * TW],
                    start=(c == 0), stop=(c == NCHUNK - 1)),
                    waits=w_h if c == 0 else [], inc="pe" if c == NCHUNK - 1 else None)
            w_hs = [("pe", last), ("sd_const", t_c)]
            if qst > 0:
                w_hs.append(("pe", tok_out[(qst - 1, NCHUNK - 1)]))
            t_hs = p.emit("v", lambda e: e.tensor_scalar(
                h_sb[:, :], pl[:, :], coef_sb[:, 0:1], 2.0, OP.mult, OP.mult),
                waits=w_hs, inc="ve")
            t_hs_prev = t_hs

            for u in (0, 1):
                wow_load(qst, u // 2, u % 2)
            xres_load(qst, 0)

            for oc in range(NCHUNK):
                pout = pq if oc % 2 == 0 else prot
                u0, u1 = 2 * oc, 2 * oc + 1
                s0, s1 = u0 % 2, u1 % 2
                w0 = [tok_wbig[(qst, "ow", u0)],
                      ("ve", tok_attnmul[gh_last]), ("ve", t_hs)]
                if (qst, oc - 2) in tok_stt:
                    w0.append(("ve", tok_stt[(qst, oc - 2)]))
                else:
                    w0.append(("sc", tok_qevac[gh_last]))
                    w0.append(("ve", tok_m[gh_last]))
                last = 0
                for h in range(H):
                    last = p.emit("t", lambda e, s0=s0, h=h, pout=pout: e.matmul(
                        pout[:, :], wbig[s0][:, h * 128:(h + 1) * 128],
                        attn_t[:, h * TW:(h + 1) * TW],
                        start=(h == 0), stop=False),
                        waits=w0 if h == 0 else [], inc="pe" if h == H - 1 else None)
                prev_use_slot[s0] = last
                if qst == 0:
                    wbk_ow[u0] = p.emit(
                        "y", dma(wow_bf[oc, :, 0:E], wbig[s0][:, :]),
                        waits=[tok_wbig[(qst, "ow", u0)]],
                        inc=f"sd_wbk{s0}", inc_amt=16)
                    prev_wbk_slot[s0] = wbk_ow[u0]
                if u0 + 2 < 2 * NCHUNK:
                    wow_load(qst, (u0 + 2) // 2, (u0 + 2) % 2)

                w1 = [tok_wbig[(qst, "ow", u1)]]
                lastw = 0
                for c in range(NCHUNK):
                    lastw = p.emit("t", lambda e, s1=s1, c=c, pout=pout: e.matmul(
                        pout[:, :], wbig[s1][:, c * 128:(c + 1) * 128],
                        xn_sub[:, c * TW:(c + 1) * TW],
                        start=False, stop=False),
                        waits=w1 if c == 0 else [], inc="pe" if c == NCHUNK - 1 else None)
                prev_use_slot[s1] = lastw
                if qst == 0:
                    wbk_ow[u1] = p.emit(
                        "y", dma(wow_bf[oc, :, E:2 * E], wbig[s1][:, :]),
                        waits=[tok_wbig[(qst, "ow", u1)]],
                        inc=f"sd_wbk{s1}", inc_amt=16)
                    prev_wbk_slot[s1] = wbk_ow[u1]
                if u1 + 2 < 2 * NCHUNK:
                    wow_load(qst, (u1 + 2) // 2, (u1 + 2) % 2)

                tok_out[(qst, oc)] = p.emit("t", lambda e, oc=oc, pout=pout: e.matmul(
                    pout[:, :], wb_s[:, oc * 128:(oc + 1) * 128],
                    h_sb[:, :], start=False, stop=True),
                    waits=[("sd_lora", t_lora), ("ve", t_hs)], inc="pe")

                w_stt = [("pe", tok_out[(qst, oc)]),
                         ("sd_xres", tok_xres[(qst, oc)]),
                         ("sd_const", t_c)]
                if (qst, oc - 1) in tok_store:
                    w_stt.append(("sd_out", tok_store[(qst, oc - 1)]))
                elif qst > 0:
                    w_stt.append(("sd_out", tok_store[(qst - 1, NCHUNK - 1)]))
                tok_stt[(qst, oc)] = p.emit(
                    "v", lambda e, oc=oc, pout=pout: e.scalar_tensor_tensor(
                        out_sb[:, :], pout[:, :], bias_sb[:, oc:oc + 1],
                        x_res[:, :], OP.add, OP.add),
                    waits=w_stt, inc="ve")
                tok_store[(qst, oc)] = p.emit(
                    "y", lambda e, oc=oc, qst=qst: e.dma_start(
                        outT[oc * 128:(oc + 1) * 128, qst * TW:(qst + 1) * TW],
                        out_sb[:, :]),
                    waits=[("ve", tok_stt[(qst, oc)])], inc="sd_out", inc_amt=16)
                if oc + 1 < NCHUNK:
                    xres_load(qst, oc + 1)
            prev_out_last = tok_out[(qst, NCHUNK - 1)]

        p.emit("y", lambda e: e.nop(),
               waits=[("sd_out", tok_store[(NQST - 1, NCHUNK - 1)])])

        sem_stack = contextlib.ExitStack()
        with sem_stack:
            sems = {}
            for name in list(p.count):
                sems[name] = sem_stack.enter_context(nc.semaphore(name))
            with nc.Block() as block:
                block.tensor(p.runner("t", sems))
                block.vector(p.runner("v", sems))
                block.scalar(p.runner("s", sems))
                block.gpsimd(p.runner("g", sems))
                block.sync(p.runner("y", sems))
    return nc



def _prep_inputs(L0_attn_hidden_states, L0_attn_Wq, L0_attn_Wk, L0_attn_Wv,
                 L0_attn_Wo, L0_ffn_lora_A, L0_ffn_lora_B, L0_ffn_W,
                 L0_ffn_bias, L0_ffn_lora_weights):
    f32 = np.float32
    x = np.asarray(L0_attn_hidden_states)
    Wq = np.asarray(L0_attn_Wq); Wk = np.asarray(L0_attn_Wk)
    Wv = np.asarray(L0_attn_Wv); Wo = np.asarray(L0_attn_Wo)
    A = np.asarray(L0_ffn_lora_A); B = np.asarray(L0_ffn_lora_B)
    W = np.asarray(L0_ffn_W); bias = np.asarray(L0_ffn_bias)
    lw = np.asarray(L0_ffn_lora_weights)

    com = {}
    com["wq"] = np.ascontiguousarray(Wq.reshape(H, 128, NCHUNK, 128).transpose(0, 3, 2, 1)).reshape(H, 128, E)
    com["wk"] = np.ascontiguousarray(Wk.reshape(KV, 128, NCHUNK, 128).transpose(0, 3, 2, 1)).reshape(KV, 128, E)
    com["wvt"] = np.ascontiguousarray(Wv.T).reshape(NCHUNK, 128, 1024)
    com["wo"] = np.ascontiguousarray(Wo.reshape(NCHUNK, 128, H, 128).transpose(0, 3, 2, 1)).reshape(NCHUNK, 128, E)
    com["wf"] = np.ascontiguousarray(W.reshape(NCHUNK, 128, NCHUNK, 128).transpose(0, 3, 2, 1)).reshape(NCHUNK, 128, E)
    A_all = A.reshape(128, E)
    com["wa"] = np.ascontiguousarray(
        A_all.T.reshape(NCHUNK, 128, 128).transpose(1, 0, 2)).reshape(128, E)
    com["wb"] = np.ascontiguousarray(B.transpose(0, 2, 1).reshape(128, E))
    com["bias_t"] = np.ascontiguousarray(bias.reshape(NCHUNK, 128).T)
    com["coef"] = np.ascontiguousarray(np.repeat(lw, 16).reshape(128, 1))

    R = np.zeros((128, 128), dtype=f32)
    for i in range(64):
        R[i, i + 64] = -1.0
        R[i + 64, i] = 1.0
    com["rt"] = np.ascontiguousarray(R.T.astype(ml_dtypes.bfloat16))

    kl = np.arange(128)[:, None]
    ql = np.arange(TW)[None, :]
    m = np.zeros((8, 128, TW), dtype=f32)
    for j in range(4):
        m[j] = (ql <= kl + j * 128)
        m[4 + j] = (ql >= kl + j * 128)
    com["masks"] = np.ascontiguousarray(
        m.transpose(1, 0, 2).reshape(128, 8 * TW).astype(np.uint8))

    d = np.arange(128, dtype=np.float64) % 64
    invfreq = 10000.0 ** (-d / 64.0)
    t = np.arange(S, dtype=np.float64)
    freqs = invfreq[:, None] * t[None, :]
    com["k1"] = np.ascontiguousarray(np.round(freqs / (2 * np.pi)).astype(f32))
    com["k2"] = np.ascontiguousarray(np.round((freqs + np.pi / 2) / (2 * np.pi)).astype(f32))

    per_core = [np.ascontiguousarray(x[b].T) for b in range(x.shape[0])]
    return com, per_core


_NC_CACHE = {}


def _get_nc():
    if "nc" not in _NC_CACHE:
        _NC_CACHE["nc"] = build_program()
    return _NC_CACHE["nc"]


def kernel(**inputs):
    com, per_core = _prep_inputs(**inputs)
    nc = _get_nc()
    in_maps = []
    for b in range(len(per_core)):
        m = dict(com)
        m["xT"] = per_core[b]
        in_maps.append(m)
    res = run_bass_kernel_spmd(nc, in_maps, core_ids=list(range(len(per_core))))
    outs = [np.ascontiguousarray(np.asarray(r["outT"]).T.astype(np.float32))
            for r in res.results]
    return np.stack(outs, axis=0)
